# version 18
# baseline (speedup 1.0000x reference)
import numpy as np
from contextlib import ExitStack

import concourse.bass as bass
import concourse.mybir as mybir
from concourse.bass_utils import run_bass_kernel_spmd

F32 = mybir.dt.float32
BF16 = mybir.dt.bfloat16
AF = mybir.ActivationFunctionType
ALU = mybir.AluOpType
AXX = mybir.AxisListType.X

ENGS = ("sync", "tensor", "vector", "scalar", "gpsimd")

B, S, D, DK = 4, 4096, 1024, 8
TOK = 2048
PE_DIM = 16
F, FC = 4096, 32
E = 8
WA = 256
NTA = TOK // WA
EPS = 1e-5


class Prog:

    def __init__(self, nc, ctx):
        self.nc = nc
        self.ctx = ctx
        self.sems = {}
        self.counts = {}
        self.queues = {e: [] for e in ENGS}
        self.last_waited = {}
        for e in ENGS:
            self._chan(e)

    def _chan(self, name):
        if name not in self.sems:
            self.sems[name] = self.ctx.enter_context(
                self.nc.semaphore("sem_" + name))
            self.counts[name] = 0
        return name

    def s(self, eng, deps, fn, dma=0, chan=None):
        chan = self._chan(chan or eng)
        waits = []
        flat = []
        for d in deps:
            if not d:
                continue
            if isinstance(d, tuple) and len(d) == 2 and isinstance(d[0], str):
                flat.append(d)
            else:
                flat.extend(x for x in d if x)
        for te, tc in flat:
            if tc <= 0:
                continue
            key = (eng, te)
            if tc <= self.last_waited.get(key, 0):
                continue
            self.last_waited[key] = tc
            waits.append((self.sems[te], tc))
        inc = 16 * dma if dma else 1
        sem = self.sems[chan]
        self.queues[eng].append((waits, fn, sem, inc, dma))
        self.counts[chan] += inc
        return (chan, self.counts[chan])

    def snap(self):
        return dict(self.counts)

    def all_deps(self, snapshot=None):
        snapshot = snapshot or self.counts
        return [(e, v) for e, v in snapshot.items()]

    def emit(self, block):
        def make(engname):
            def body(q):
                for waits, fn, sem, inc, dma in self.queues[engname]:
                    for s, v in waits:
                        q.wait_ge(s, v)
                    res = fn(q)
                    if dma:
                        assert isinstance(res, (list, tuple)) and len(res) == dma
                        for instr in res:
                            instr.then_inc(sem, 16)
                    else:
                        res.then_inc(sem, 1)
            return body

        block.sync(make("sync"))
        block.tensor(make("tensor"))
        block.vector(make("vector"))
        block.scalar(make("scalar"))
        block.gpsimd(make("gpsimd"))


def build_program(taps=(), upto="full"):
    nc = bass.Bass("TRN2", target_bir_lowering=False, debug=False)

    def inp(name, shape, dtype=F32):
        return nc.dram_tensor(name, list(shape), dtype, kind="ExternalInput").ap()

    xT = inp("xT", (D, TOK))
    peT = inp("peT", (PE_DIM, TOK))
    Wdx_m = inp("Wdx_m", (D, D))
    Wdx_p = inp("Wdx_p", (PE_DIM, D))
    Wx_m = inp("Wx_m", (D, D))
    Wx_p = inp("Wx_p", (PE_DIM, D))
    Wgx = inp("Wgx", (D, D))
    Wga = inp("Wga", (D, D))
    Wo = inp("Wo", (D, D))
    mixw = inp("mixw", (D, D))
    rw = inp("rw", (D, E))
    bdx = inp("bdx", (128, DK))
    bx = inp("bx", (128, DK))
    bgx = inp("bgx", (128, DK))
    bga = inp("bga", (128, DK))
    bo = inp("bo", (128, DK))
    mixb = inp("mixb", (128, DK))
    alog = inp("alog", (128, DK))
    g1 = inp("g1", (128, DK))
    be1 = inp("be1", (128, DK))
    g2 = inp("g2", (128, DK))
    be2 = inp("be2", (128, DK))
    b1h = inp("b1h", (128, E * FC))
    b2h = inp("b2h", (E, D))
    w1h = inp("w1h", (E, 16, 128, DK, 2, 128))
    w2h = inp("w2h", (E, 2, 128, 16, D))
    me = inp("me", (128, 1))
    mo = inp("mo", (128, 1))
    ident = inp("ident", (128, 128))

    OUT = nc.dram_tensor("OUT", [TOK, D], F32, kind="ExternalOutput").ap()
    APSP = nc.dram_tensor("APSP", [128, DK * TOK], F32, kind="Internal").ap()
    ACSP = nc.dram_tensor("ACSP", [128, DK * TOK], BF16, kind="Internal").ap()
    CRSP = nc.dram_tensor("CRSP", [128, DK * TOK], F32, kind="Internal").ap()
    CIN = nc.dram_tensor("CIN", [1, D], F32, kind="Internal").ap()
    COUT = nc.dram_tensor("COUT", [1, D], F32, kind="Internal").ap()

    tap_outs = {}
    for t in taps:
        shp, dt = {"APSP": ([128, DK * TOK], F32), "CRSP": ([128, DK * TOK], F32),
                   "N2": ([128, DK * TOK], BF16), "CMB": ([128, 16 * E], F32),
                   "OACC": ([128, 16 * D], F32),
                   "XH": ([128, DK * TOK], F32), "AA": ([128, DK * TOK], F32),
                   "CC": ([128, DK * TOK], F32), "HS": ([128, DK * TOK], BF16),
                   }[t]
        tap_outs[t] = nc.dram_tensor("DBG_" + t, shp, dt, kind="ExternalOutput").ap()

    ctx = ExitStack()
    sb = lambda name, shape, dtype=F32: ctx.enter_context(
        nc.sbuf_tensor(name, list(shape), dtype))

    g_bdx = sb("g_bdx", (128, DK)); g_bx = sb("g_bx", (128, DK))
    g_bgx = sb("g_bgx", (128, DK)); g_bga = sb("g_bga", (128, DK))
    g_bo = sb("g_bo", (128, DK)); g_mixb = sb("g_mixb", (128, DK))
    g_alog = sb("g_alog", (128, DK)); g_g1 = sb("g_g1", (128, DK))
    g_be1 = sb("g_be1", (128, DK)); g_g2 = sb("g_g2", (128, DK))
    g_be2 = sb("g_be2", (128, DK))
    g_ae = sb("g_ae", (128, DK)); g_saq = sb("g_saq", (128, DK))
    g_bhalf = sb("g_bhalf", (128, DK))
    g_bsaq = sb("g_bsaq", (128, DK))
    g_b1 = sb("g_b1", (128, E * FC))
    g_b2bf = sb("g_b2bf", (E, D), BF16)
    g_rw = sb("g_rw", (128, DK * E))
    g_me = sb("g_me", (128, 1)); g_mo = sb("g_mo", (128, 1))
    g_ident = sb("g_ident", (128, 128))
    g_H = sb("g_H", (128, DK))
    g_hc = sb("g_hc", (128, DK))
    g_ar = sb("g_ar", (128, DK))
    g_cp = sb("g_cp", (128, 4))
    g_ones1 = sb("g_ones1", (1, 128))
    g_onescol = sb("g_onescol", (128, 1), BF16)
    g_onesW = sb("g_onesW", (128, WA))

    p = Prog(nc, ctx)
    V, Sc, T, Sy, G = "vector", "scalar", "tensor", "sync", "gpsimd"

    smalls = [(g_bdx, bdx), (g_bx, bx), (g_bgx, bgx), (g_bga, bga), (g_bo, bo),
              (g_mixb, mixb), (g_alog, alog), (g_g1, g1), (g_be1, be1),
              (g_g2, g2), (g_be2, be2), (g_b1, b1h), (g_me, me), (g_mo, mo),
              (g_ident, ident), (g_rw.rearrange("p (k e) -> p k e", k=DK), rw.rearrange("(k p) e -> p k e", p=128))]
    def _ap(x):
        return x if isinstance(x, bass.AP) else x[:]
    ld0 = p.s(Sy, [], lambda q, s=smalls: [
        q.dma_start(out=_ap(dst), in_=src) for dst, src in s], dma=len(smalls), chan="c_ld0")
    ldb2 = p.s(G, [], lambda q: [q.dma_start(out=g_b2bf[:], in_=b2h)], dma=1, chan="c_b2")

    cst = p.s(V, [], lambda q: (
        q.memset(g_cp[:, 0:1], 1.0),
        q.memset(g_cp[:, 1:2], EPS),
        q.memset(g_ones1[:], 1.0),
        q.memset(g_onescol[:], 1.0),
        q.memset(g_onesW[:], 1.0),
        q.memset(g_hc[:], 0.0),
        q.memset(g_ar[:], 1.0))[-1])
    _cut0 = (upto == "ldA")
    ae_s = p.s(Sc, [ld0], lambda q: q.activation(
        out=g_ae[:], in_=g_alog[:], func=AF.Sigmoid))
    sq_s = p.s(Sc, [ae_s], lambda q: q.activation(
        out=g_saq[:], in_=g_ae[:], func=AF.Square))
    saq_s = p.s(Sc, [sq_s, cst], lambda q: q.activation(
        out=g_saq[:], in_=g_saq[:], func=AF.Sqrt, scale=-1.0, bias=g_cp[:, 0:1]))
    bh_v0 = p.s(V, [ld0, cst], lambda q: q.tensor_tensor(
        out=g_bhalf[:], in0=g_bdx[:], in1=g_bo[:], op=ALU.add))
    bh_v = p.s(V, [bh_v0], lambda q: q.tensor_scalar(
        out=g_bhalf[:], in0=g_bhalf[:], scalar1=0.5, scalar2=None, op0=ALU.mult))
    bsaq_v = p.s(V, [saq_s, bh_v], lambda q: q.tensor_tensor(
        out=g_bsaq[:], in0=g_bx[:], in1=g_saq[:], op=ALU.mult))

    _cut1 = (upto == "ldB")
    ctxA = ExitStack()
    sbA = lambda name, shape, dtype=F32: ctxA.enter_context(
        nc.sbuf_tensor(name, list(shape), dtype))
    psA = lambda name: ctxA.enter_context(nc.psum_tensor(name, [128, 512], F32))

    a_wdx = sbA("a_wdx", (128, DK * D), BF16)
    a_wx = sbA("a_wx", (128, DK * D), BF16)
    a_wdxp = sbA("a_wdxp", (PE_DIM, D), BF16)
    a_wxp = sbA("a_wxp", (PE_DIM, D), BF16)
    a_wgx = sbA("a_wgx", (128, DK * D), BF16)
    a_wga = sbA("a_wga", (128, DK * D), BF16)
    a_wo = sbA("a_wo", (128, DK * D), BF16)
    a_pebf = sbA("a_pebf", (PE_DIM, TOK), BF16)
    a_xf = [sbA(f"a_xf{i}", (128, DK * WA)) for i in range(2)]
    a_xbf = [sbA(f"a_xbf{i}", (128, DK * WA), BF16) for i in range(2)]
    a_xhf = sbA("a_xhf", (128, DK * WA))
    a_xhbf = sbA("a_xhbf", (128, DK * WA), BF16)
    a_a = sbA("a_a", (128, DK * WA))
    a_xg = sbA("a_xg", (128, DK * WA))
    a_c = sbA("a_c", (128, DK * WA))
    a_hbf = [sbA(f"a_hbf{i}", (128, DK * WA), BF16) for i in range(2)]
    a_acbf = [sbA(f"a_acbf{i}", (128, DK * WA), BF16) for i in range(2)]
    a_apre = sbA("a_apre", (128, DK * WA))
    a_gt = sbA("a_gt", (128, 2 * WA))
    a_st = sbA("a_st", (128, WA))
    a_scf = sbA("a_scf", (128, 2 * WA))
    a_hcm = sbA("a_hcm", (128, DK))
    a_corrbf = sbA("a_corrbf", (128, DK * WA), BF16)
    a_corrf = sbA("a_corrf", (128, DK * WA))

    pXH = [psA("pXH0"), psA("pXH1")]
    pGA = [psA("pGA0"), psA("pGA1")]
    pGX = [psA("pGX0"), psA("pGX1")]
    pDW = [psA("pDW0"), psA("pDW1")]

    if upto in ("ldA", "ldB"):
        wload = ldb2
    else:
        wload = p.s(G, [ldb2], lambda q: [
            q.dma_start(out=a_wdx[:].rearrange("p (k n) -> p k n", k=DK), in_=Wdx_m.rearrange("(k p) n -> p k n", p=128)),
            q.dma_start(out=a_wx[:].rearrange("p (k n) -> p k n", k=DK), in_=Wx_m.rearrange("(k p) n -> p k n", p=128)),
            q.dma_start(out=a_wdxp[:], in_=Wdx_p),
            q.dma_start(out=a_wxp[:], in_=Wx_p),
            q.dma_start(out=a_wgx[:].rearrange("p (k n) -> p k n", k=DK), in_=Wgx.rearrange("(k p) n -> p k n", p=128)),
            q.dma_start(out=a_wga[:].rearrange("p (k n) -> p k n", k=DK), in_=Wga.rearrange("(k p) n -> p k n", p=128)),
            q.dma_start(out=a_wo[:].rearrange("p (k n) -> p k n", k=DK), in_=Wo.rearrange("(k p) n -> p k n", p=128)),
            q.dma_start(out=a_pebf[:], in_=peT),
        ], dma=8, chan="c_wl")

    xT3 = xT.rearrange("(k p) t -> p k t", p=128)
    AP3 = APSP.rearrange("p (k t) -> p k t", k=DK)
    AC3 = ACSP.rearrange("p (k t) -> p k t", k=DK)
    CR3 = CRSP.rearrange("p (k t) -> p k t", k=DK)

    xf_free = [None, None]
    xh_bank_free = [None, None]
    last_dxwo_mm = None
    last_gate_mm = None
    last_vx = None
    ga_bank_free = [None, None]
    gx_bank_free = [None, None]
    dw_bank_free = [None, None]
    spill_prev = None
    scan_done = []

    cW = lambda j: (slice(None), slice(j * WA, (j + 1) * WA))

    fstates = []

    def front(i):
        nonlocal_state = {}
        t0 = i * WA
        bsel = i % 2
        xf = a_xf[bsel]
        xbf = a_xbf[bsel]
        hbf = a_hbf[bsel]
        acbf = a_acbf[bsel]
        ldx = p.s(Sy, [st_.get("xf_free%d" % bsel)], lambda q, t0=t0, xf=xf: [
            q.dma_start(out=xf[:].rearrange("p (k t) -> p k t", k=DK),
                        in_=xT3[:, :, t0:t0 + WA])], dma=1, chan=f"c_x{bsel}")
        cast_x = p.s(Sc, [ldx, st_.get("last_dxwo_mm")],
                     lambda q, xf=xf, xbf=xbf: q.copy(out=xbf[:], in_=xf[:]))
        cast_x_dep = [cast_x, wload] if i == 0 else [cast_x]

        evac_xh = []
        for j in range(DK):
            bk = j % 2
            mm = p.s(T, cast_x_dep + [st_.get("xh_bank_free%d" % bk)],
                     lambda q, j=j, bk=bk, t0=t0, xbf=xbf: (
                         [q.matmul(pXH[bk][:, 0:WA],
                                   a_wx[:, k * D + j * 128:k * D + j * 128 + 128],
                                   xbf[cW(k)], start=(k == 0), stop=False)
                          for k in range(DK)],
                         q.matmul(pXH[bk][:, 0:WA], a_wxp[:, j * 128:(j + 1) * 128],
                                  a_pebf[:, t0:t0 + WA], start=False,
                                  stop=True))[-1])
            ev = p.s(Sc, [mm, bsaq_v, st_.get("last_vx")],
                     lambda q, j=j, bk=bk: q.activation(
                         out=a_xhf[cW(j)], in_=pXH[bk][:, 0:WA], func=AF.Identity,
                         scale=g_saq[:, j:j + 1], bias=g_bsaq[:, j:j + 1]))
            st_["xh_bank_free%d" % bk] = ev
            evac_xh.append(ev)
        cast_xh = p.s(G, [evac_xh[-1], st_.get("last_gate_mm")],
                      lambda q: q.tensor_copy(out=a_xhbf[:], in_=a_xhf[:]))

        prev_va, prev_vx = None, None
        axg_steps = []
        for j in range(DK):
            bk = j % 2
            mma = p.s(T, [cast_xh, st_.get("ga_bank_free%d" % bk)],
                      lambda q, j=j, bk=bk: [
                q.matmul(pGA[bk][:, 0:WA],
                         a_wga[:, k * D + j * 128:k * D + j * 128 + 128],
                         a_xhbf[cW(k)], start=(k == 0), stop=(k == DK - 1))
                for k in range(DK)][-1])
            mmx = p.s(T, [cast_xh, st_.get("gx_bank_free%d" % bk)],
                      lambda q, j=j, bk=bk: [
                q.matmul(pGX[bk][:, 0:WA],
                         a_wgx[:, k * D + j * 128:k * D + j * 128 + 128],
                         a_xhbf[cW(k)], start=(k == 0), stop=(k == DK - 1))
                for k in range(DK)][-1])
            sa = p.s(Sc, [mma, prev_va], lambda q, bk=bk, j=j: q.activation(
                out=a_gt[:, 0:WA], in_=pGA[bk][:, 0:WA], func=AF.Sigmoid,
                bias=g_bga[:, j:j + 1], scale=1.0))
            sx = p.s(Sc, [mmx, prev_vx], lambda q, bk=bk, j=j: q.activation(
                out=a_gt[:, WA:2 * WA], in_=pGX[bk][:, 0:WA], func=AF.Sigmoid,
                bias=g_bgx[:, j:j + 1], scale=1.0))
            va = p.s(V, [sa, ae_s], lambda q, j=j: q.tensor_scalar(
                out=a_a[cW(j)], in0=a_gt[:, 0:WA], scalar1=g_ae[:, j:j + 1],
                scalar2=None, op0=ALU.mult))
            vx = p.s(V, [sx], lambda q, j=j: q.tensor_tensor(
                out=a_xg[cW(j)], in0=a_gt[:, WA:2 * WA], in1=a_xhf[cW(j)],
                op=ALU.mult))
            st_["ga_bank_free%d" % bk] = sa
            st_["gx_bank_free%d" % bk] = sx
            prev_va, prev_vx = va, vx
            st_["last_gate_mm"] = mmx
            axg_steps.append((va, vx))
        st_["last_vx"] = prev_vx

        c_steps = []
        prev_cv = None
        for j in range(DK):
            va, vx = axg_steps[j]
            s1 = p.s(Sc, [va, prev_cv], lambda q, j=j: q.activation(
                out=a_st[:], in_=a_a[cW(j)], func=AF.Square))
            s2 = p.s(Sc, [s1], lambda q: q.activation(
                out=a_st[:], in_=a_st[:], func=AF.Sqrt, scale=-1.0,
                bias=g_cp[:, 0:1]))
            cv = p.s(V, [s2, vx], lambda q, j=j: q.tensor_tensor(
                out=a_c[cW(j)], in0=a_st[:], in1=a_xg[cW(j)], op=ALU.mult))
            c_steps.append(cv)
            prev_cv = cv

        scan_done = []
        prev_upc = st_.get("prev_upc")
        prev_uph = st_.get("prev_uph")
        for j in range(DK):
            sc1 = p.s(V, [c_steps[j], prev_upc, prev_uph],
                      lambda q, j=j: q.tensor_tensor_scan(
                          out=a_scf[:, 0:WA], data0=a_a[cW(j)], data1=a_c[cW(j)],
                          initial=g_hc[:, j:j + 1], op0=ALU.mult, op1=ALU.add))
            sc2 = p.s(V, [sc1, prev_upc], lambda q, j=j: q.tensor_tensor_scan(
                out=a_scf[:, WA:2 * WA], data0=a_a[cW(j)], data1=g_onesW[:],
                initial=g_ar[:, j:j + 1], op0=ALU.mult, op1=ALU.mult))
            upc = p.s(V, [sc1, sc2], lambda q, j=j: (
                q.tensor_copy(out=g_hc[:, j:j + 1], in_=a_scf[:, WA - 1:WA]),
                q.tensor_copy(out=g_ar[:, j:j + 1],
                              in_=a_scf[:, 2 * WA - 1:2 * WA]))[-1])
            uph = p.s(Sc, [sc1, st_.get("last_dxwo_mm")],
                      lambda q, j=j, hbf=hbf: q.copy(
                          out=hbf[cW(j)], in_=a_scf[:, 0:WA]))
            upa = p.s(G, [sc2, st_.get("spill_prev")],
                      lambda q, j=j, acbf=acbf: q.tensor_copy(
                          out=acbf[cW(j)], in_=a_scf[:, WA:2 * WA]))
            scan_done.append((upc, uph, upa))
            prev_upc, prev_uph = upc, uph
        st_["prev_upc"] = prev_upc
        st_["prev_uph"] = prev_uph
        return dict(i=i, t0=t0, bsel=bsel, xf=xf, xbf=xbf, hbf=hbf, acbf=acbf,
                    scan_done=scan_done)

    def back(fs):
        i, t0, bsel = fs["i"], fs["t0"], fs["bsel"]
        xf, xbf, hbf, acbf = fs["xf"], fs["xbf"], fs["hbf"], fs["acbf"]
        scan_done = fs["scan_done"]
        apre_steps = []
        for j in range(DK):
            bk = j % 2
            mm = p.s(T, [scan_done[-1][1], st_.get("dw_bank_free%d" % bk)],
                     lambda q, j=j, bk=bk, t0=t0, xbf=xbf, hbf=hbf: (
                         [q.matmul(pDW[bk][:, 0:WA],
                                   a_wdx[:, k * D + j * 128:k * D + j * 128 + 128],
                                   xbf[cW(k)], start=(k == 0), stop=False)
                          for k in range(DK)],
                         q.matmul(pDW[bk][:, 0:WA],
                                  a_wdxp[:, j * 128:(j + 1) * 128],
                                  a_pebf[:, t0:t0 + WA], start=False, stop=False),
                         [q.matmul(pDW[bk][:, 0:WA],
                                   a_wo[:, k * D + j * 128:k * D + j * 128 + 128],
                                   hbf[cW(k)], start=False, stop=(k == DK - 1))
                          for k in range(DK)])[-1][-1])
            ev0 = p.s(Sc, [mm, st_.get("spill_prev")],
                      lambda q, j=j, bk=bk: q.activation(
                          out=a_apre[cW(j)], in_=pDW[bk][:, 0:WA],
                          func=AF.Identity, scale=0.5, bias=g_bhalf[:, j:j + 1]))
            ev = p.s(V, [ev0], lambda q, j=j, xf=xf: q.tensor_tensor(
                out=a_apre[cW(j)], in0=a_apre[cW(j)], in1=xf[cW(j)], op=ALU.add))
            st_["dw_bank_free%d" % bk] = ev
            st_["last_dxwo_mm"] = mm
            apre_steps.append(ev)
        st_["xf_free%d" % bsel] = apre_steps[-1]

        sp = p.s(Sy, [apre_steps[-1], scan_done[-1][2], scan_done[-1][1],
                      st_.get("spill_prev")],
                 lambda q, t0=t0, acbf=acbf: [
                     q.dma_start(out=AP3[:, :, t0:t0 + WA],
                                 in_=a_apre[:].rearrange("p (k t) -> p k t", k=DK)),
                     q.dma_start(out=AC3[:, :, t0:t0 + WA],
                                 in_=acbf[:].rearrange("p (k t) -> p k t", k=DK)),
                 ], dma=2, chan="c_sp")
        st_["spill_prev"] = sp

    st_ = {}
    for i in range(NTA):
        fstates.append(front(i))
        if i >= 1:
            back(fstates[i - 1])
    back(fstates[NTA - 1])
    spill_prev = st_["spill_prev"]
    scan_done = fstates[-1]["scan_done"]

    hcm_v = p.s(V, [spill_prev, scan_done[-1][0], ld0], lambda q: q.tensor_scalar(
        out=a_hcm[:], in0=g_hc[:], scalar1=g_me[:], scalar2=None, op0=ALU.mult))
    cin_d = p.s(Sy, [hcm_v], lambda q: [q.dma_start(out=CIN, in_=a_hcm[:])], dma=1, chan="c_bnd")
    coll = p.s(G, [cin_d, wload], lambda q: q.collective_compute(
        "AllReduce", ALU.add, replica_groups=[[0, 1], [2, 3], [4, 5], [6, 7]],
        ins=[CIN.opt()], outs=[COUT.opt()]))
    hback = p.s(Sy, [coll], lambda q: [q.dma_start(out=g_H[:], in_=COUT)], dma=1, chan="c_bnd")
    h_v = p.s(V, [hback], lambda q: q.tensor_scalar(
        out=g_H[:], in0=g_H[:], scalar1=g_mo[:], scalar2=None, op0=ALU.mult))

    corr_prev_ev = None
    corr_sp_prev = None
    corr_ldguard = [None, None]
    cb_bank_free = [None, None]
    for i in range(NTA):
        t0 = i * WA
        par = i % 2
        ldc = p.s(Sy, [h_v, corr_ldguard[par], spill_prev],
                  lambda q, t0=t0, par=par: [
            q.dma_start(out=a_acbf[par][:].rearrange("p (k t) -> p k t", k=DK),
                        in_=AC3[:, :, t0:t0 + WA])], dma=1, chan=f"c_cl{par}")
        cbs = []
        for j in range(DK):
            cbj = p.s(V, [ldc, h_v, corr_sp_prev], lambda q, j=j, par=par:
                      q.tensor_scalar(
                out=a_corrbf[cW(j)], in0=a_acbf[par][cW(j)],
                scalar1=g_H[:, j:j + 1], scalar2=None, op0=ALU.mult))
            cbs.append(cbj)
        corr_ldguard[par] = cbs[-1]
        evs = []
        for j in range(DK):
            bk = j % 2
            mm = p.s(T, [cbs[-1], cb_bank_free[bk]], lambda q, j=j, bk=bk: [
                q.matmul(pDW[bk][:, 0:WA],
                         a_wo[:, k * D + j * 128:k * D + j * 128 + 128],
                         a_corrbf[cW(k)], start=(k == 0), stop=(k == DK - 1))
                for k in range(DK)][-1])
            ev = p.s(V, [mm], lambda q, j=j, bk=bk: q.tensor_scalar(
                out=a_corrf[cW(j)], in0=pDW[bk][:, 0:WA], scalar1=0.5,
                scalar2=None, op0=ALU.mult))
            cb_bank_free[bk] = ev
            evs.append(ev)
        spc = p.s(Sy, [evs[-1]], lambda q, t0=t0: [
            q.dma_start(out=CR3[:, :, t0:t0 + WA],
                        in_=a_corrf[:].rearrange("p (k t) -> p k t", k=DK))], dma=1, chan="c_cs")
        corr_prev_ev = evs[-1]
        corr_sp_prev = spc

    if "CRSP" in tap_outs:
        corr_sp_prev = p.s(Sy, [corr_sp_prev], lambda q: [
            q.dma_start(out=tap_outs["CRSP"], in_=CRSP)], dma=1, chan="c_tap")
    if "APSP" in tap_outs:
        corr_sp_prev = p.s(Sy, [corr_sp_prev, spill_prev], lambda q: [
            q.dma_start(out=tap_outs["APSP"], in_=APSP)], dma=1, chan="c_tap")

    snapA = p.snap()
    barrierA = p.all_deps(snapA)
    ctxA.close()

    ctxB = ExitStack()
    sbB = lambda name, shape, dtype=F32: ctxB.enter_context(
        nc.sbuf_tensor(name, list(shape), dtype))
    g_combine = sbB("g_combine", (128, (TOK // 128) * E))
    n2bf = sbB("n2bf", (128, DK * TOK), BF16)
    out_acc = sbB("out_acc", (128, (TOK // 128) * D))
    ctxB1 = ExitStack()
    sbB1 = lambda name, shape, dtype=F32: ctxB1.enter_context(
        nc.sbuf_tensor(name, list(shape), dtype))
    b_mix = sbB1("b_mix", (128, DK * D), BF16)
    b_ap = sbB1("b_ap", (128, DK * WA))
    b_crr = sbB1("b_crr", (128, DK * WA))
    b_stats = sbB1("b_stats", (128, 2 * DK * WA), BF16)
    b_x1 = sbB1("b_x1", (128, DK * WA))
    b_n1 = sbB1("b_n1", (128, DK * WA))
    b_n1bf = sbB1("b_n1bf", (128, DK * WA), BF16)
    b_x2 = sbB1("b_x2", (128, DK * WA))
    b_n2f = sbB1("b_n2f", (128, DK * WA))
    b_rows = sbB1("b_rows", (1, 8 * WA))
    b_zr = sbB1("b_zr", (E, WA))
    b_ztm = sbB1("b_ztm", (128, 2 * E))
    b_srt = sbB1("b_srt", (128, 2 * E))
    b_sgn = sbB1("b_sgn", (128, E))
    b_ez = sbB1("b_ez", (128, 2 * E))
    b_nm = sbB1("b_nm", (128, 4))
    b_cebf = sbB1("b_cebf", (E, 2 * 128), BF16)

    ctxBp = ExitStack()
    psB = lambda name, shape: ctxBp.enter_context(
        nc.psum_tensor(name, list(shape), F32))
    pA_ = psB("pA_", [128, 512])
    pB_ = psB("pB_", [128, 512])
    pRow = psB("pRow", [128, 512])
    pBC = psB("pBC", [128, 512])
    pBC2 = psB("pBC2", [128, 512])
    pT_ = psB("pT_", [128, 512])
    pB2 = psB("pB2", [128, 1024])

    def finish_dummy(dep, pre_close=()):
        for c in pre_close:
            c.close()
        mz = p.s(V, [dep] + p.all_deps(), lambda q: q.memset(out_acc[:], 0.0))
        ods = []
        prev = mz
        for blk in range(TOK // 128):
            od = p.s(Sy, [prev], lambda q, blk=blk: [
                q.dma_start(out=OUT[blk * 128:(blk + 1) * 128, :],
                            in_=out_acc[:, blk * D:(blk + 1) * D])], dma=1,
                chan="c_out")
            prev = od
            ods.append(od)
        p.s(Sy, [ods[-1]], lambda q: q.nop())
        block = ctx.enter_context(nc.Block())
        p.emit(block)
        ctxB.close()
        ctx.close()
        return nc

    if upto in ("ld", "ldA", "ldB"):
        return finish_dummy(None, (ctxBp, ctxB1))
    if upto == "A":
        return finish_dummy(None, (ctxBp, ctxB1))

    mix_ld = p.s(G, barrierA, lambda q: [
        q.dma_start(out=b_mix[:].rearrange("p (k n) -> p k n", k=DK), in_=mixw.rearrange("(k p) n -> p k n", p=128))],
        dma=1, chan="c_mx")

    ap_prev = None
    n2_last = None
    comb_last = None
    oacc_last = None
    pT_free = None
    pB2_free = None
    pBC_free = None
    pBC2_free = None
    pRow_free = None
    mixbank_free = [None, None]
    for i in range(NTA):
        t0 = i * WA
        ldap = p.s(Sy, ([ap_prev] if ap_prev else []) + (barrierA if i == 0 else []),
                   lambda q, t0=t0: [
            q.dma_start(out=b_ap[:].rearrange("p (k t) -> p k t", k=DK),
                        in_=AP3[:, :, t0:t0 + WA]),
            q.dma_start(out=b_crr[:].rearrange("p (k t) -> p k t", k=DK),
                        in_=CR3[:, :, t0:t0 + WA])], dma=2, chan="c_b1l")
        apc = p.s(V, [ldap], lambda q: q.tensor_tensor(
            out=b_ap[:], in0=b_ap[:], in1=b_crr[:], op=ALU.add))
        apbf = p.s(Sc, [apc], lambda q: q.copy(
            out=b_stats[:, 0:DK * WA], in_=b_ap[:]))
        apsq = p.s(Sc, [apc], lambda q: q.activation(
            out=b_stats[:, DK * WA:2 * DK * WA], in_=b_ap[:], func=AF.Square))

        def ln_stats(dep_sq, dep_bf):
            st3 = b_stats[:].rearrange("p (h k t) -> p k h t", h=2, k=DK)
            pview = pRow[0:1, 0:2 * WA].rearrange("p (h t) -> p h t", h=2)
            return p.s(T, [dep_bf, dep_sq, pRow_free], lambda q: [
                q.matmul(pview, g_onescol[:], st3[:, k],
                         start=(k == 0), stop=(k == DK - 1)) for k in range(DK)][-1])

        st0 = ln_stats(apsq, apbf)
        r1a = p.s(V, [st0], lambda q: q.tensor_scalar(
            out=b_rows[:, 0:2 * WA], in0=pRow[0:1, 0:2 * WA],
            scalar1=1.0 / D, scalar2=None, op0=ALU.mult))
        r1b_ = p.s(V, [r1a], lambda q: q.tensor_tensor(
            out=b_rows[:, 6 * WA:7 * WA], in0=b_rows[:, 0:WA],
            in1=b_rows[:, 0:WA], op=ALU.mult))
        r1 = p.s(V, [r1b_], lambda q: q.tensor_tensor(
            out=b_rows[:, 2 * WA:3 * WA], in0=b_rows[:, WA:2 * WA],
            in1=b_rows[:, 6 * WA:7 * WA], op=ALU.subtract))
        pRow_free = r1a
        r2a = p.s(Sc, [r1, pBC_free], lambda q: q.activation(
            out=b_rows[:, 3 * WA:4 * WA], in_=b_rows[:, 2 * WA:3 * WA],
            func=AF.Ln, bias=g_cp[0:1, 1:2], scale=1.0))
        r2 = p.s(Sc, [r2a], lambda q: q.activation(
            out=b_rows[:, 3 * WA:4 * WA], in_=b_rows[:, 3 * WA:4 * WA],
            func=AF.Exp, scale=-0.5))
        r3 = p.s(V, [r2, pBC2_free], lambda q: q.tensor_tensor(
            out=b_rows[:, 4 * WA:5 * WA], in0=b_rows[:, 0:WA],
            in1=b_rows[:, 3 * WA:4 * WA], op=ALU.mult))
        r3x = p.s(V, [r3], lambda q: q.tensor_tensor(
            out=b_rows[:, 6 * WA:7 * WA], in0=b_rows[:, 3 * WA:4 * WA],
            in1=b_rows[:, 3 * WA:4 * WA], op=ALU.mult))
        r3y = p.s(V, [r3x], lambda q: q.tensor_tensor(
            out=b_rows[:, 6 * WA:7 * WA], in0=b_rows[:, 6 * WA:7 * WA],
            in1=b_rows[:, 2 * WA:3 * WA], op=ALU.mult))
        r4a = p.s(Sc, [r3y], lambda q: q.activation(
            out=b_rows[:, 5 * WA:6 * WA], in_=b_rows[:, 6 * WA:7 * WA],
            func=AF.Ln, bias=g_cp[0:1, 1:2], scale=1.0))
        r4 = p.s(Sc, [r4a], lambda q: q.activation(
            out=b_rows[:, 5 * WA:6 * WA], in_=b_rows[:, 5 * WA:6 * WA],
            func=AF.Exp, scale=-0.5))
        bc1 = p.s(T, [r3, pBC_free], lambda q: q.matmul(
            pBC[:, 0:2 * WA], g_ones1[:], b_rows[:, 3 * WA:5 * WA],
            start=True, stop=True))
        bc2 = p.s(T, [r4, pBC2_free], lambda q: q.matmul(
            pBC2[:, 0:WA], g_ones1[:], b_rows[:, 5 * WA:6 * WA],
            start=True, stop=True))
        x1s, n1s = [], []
        for j in range(DK):
            x1a = p.s(V, [bc1, apc], lambda q, j=j: q.tensor_tensor(
                out=b_x1[cW(j)], in0=b_ap[cW(j)], in1=pBC[:, 0:WA], op=ALU.mult))
            x1j = p.s(V, [x1a], lambda q, j=j: q.tensor_tensor(
                out=b_x1[cW(j)], in0=b_x1[cW(j)], in1=pBC[:, WA:2 * WA],
                op=ALU.subtract))
            n1a = p.s(V, [x1j, bc2], lambda q, j=j: q.tensor_tensor(
                out=b_n1[cW(j)], in0=b_x1[cW(j)], in1=pBC2[:, 0:WA], op=ALU.mult))
            n1j = p.s(V, [n1a], lambda q, j=j: q.tensor_scalar(
                out=b_n1[cW(j)], in0=b_n1[cW(j)], scalar1=g_g1[:, j:j + 1],
                scalar2=g_be1[:, j:j + 1], op0=ALU.mult, op1=ALU.add))
            x1s.append(x1j)
            n1s.append(n1j)
        pBC_free = n1s[-1]
        pBC2_free = n1s[-1]
        n1bf = p.s(Sc, [n1s[-1]], lambda q: q.copy(out=b_n1bf[:], in_=b_n1[:]))
        x2s = []
        for j in range(DK):
            bk = j % 2
            pm = [pA_, pB_][bk]
            mm = p.s(T, [n1bf, mixbank_free[bk], mix_ld], lambda q, j=j, pm=pm: [
                q.matmul(pm[:, 0:WA], b_mix[:, k * D + j * 128:k * D + j * 128 + 128],
                         b_n1bf[cW(k)], start=(k == 0), stop=(k == DK - 1))
                for k in range(DK)][-1])
            x2a = p.s(V, [mm], lambda q, j=j, pm=pm: q.scalar_tensor_tensor(
                out=b_x2[cW(j)], in0=pm[:, 0:WA], scalar=g_mixb[:, j:j + 1],
                in1=b_n1[cW(j)], op0=ALU.add, op1=ALU.add))
            x2j = p.s(V, [x2a], lambda q, j=j: q.tensor_tensor(
                out=b_x2[cW(j)], in0=b_x2[cW(j)], in1=b_x1[cW(j)], op=ALU.add))
            mixbank_free[bk] = x2j
            x2s.append(x2j)
        x2bf = p.s(Sc, [x2s[-1], st0], lambda q: q.copy(
            out=b_stats[:, 0:DK * WA], in_=b_x2[:]))
        x2sq = p.s(Sc, [x2s[-1], st0], lambda q: q.activation(
            out=b_stats[:, DK * WA:2 * DK * WA], in_=b_x2[:], func=AF.Square))
        st0b = ln_stats(x2sq, x2bf)
        r1b0 = p.s(V, [st0b], lambda q: q.tensor_scalar(
            out=b_rows[:, 0:2 * WA], in0=pRow[0:1, 0:2 * WA],
            scalar1=1.0 / D, scalar2=None, op0=ALU.mult))
        r1b1 = p.s(V, [r1b0], lambda q: q.tensor_tensor(
            out=b_rows[:, 6 * WA:7 * WA], in0=b_rows[:, 0:WA],
            in1=b_rows[:, 0:WA], op=ALU.mult))
        r1b = p.s(V, [r1b1], lambda q: q.tensor_tensor(
            out=b_rows[:, 2 * WA:3 * WA], in0=b_rows[:, WA:2 * WA],
            in1=b_rows[:, 6 * WA:7 * WA], op=ALU.subtract))
        pRow_free = r1b0
        r2b0 = p.s(Sc, [r1b, bc1, pBC_free], lambda q: q.activation(
            out=b_rows[:, 3 * WA:4 * WA], in_=b_rows[:, 2 * WA:3 * WA],
            func=AF.Ln, bias=g_cp[0:1, 1:2], scale=1.0))
        r2b = p.s(Sc, [r2b0], lambda q: q.activation(
            out=b_rows[:, 3 * WA:4 * WA], in_=b_rows[:, 3 * WA:4 * WA],
            func=AF.Exp, scale=-0.5))
        r3b = p.s(V, [r2b, bc1], lambda q: q.tensor_tensor(
            out=b_rows[:, 4 * WA:5 * WA], in0=b_rows[:, 0:WA],
            in1=b_rows[:, 3 * WA:4 * WA], op=ALU.mult))
        bc1b = p.s(T, [r3b, pBC_free], lambda q: q.matmul(
            pBC[:, 0:2 * WA], g_ones1[:], b_rows[:, 3 * WA:5 * WA],
            start=True, stop=True))
        n2s = []
        for j in range(DK):
            n2a = p.s(V, [bc1b, x2s[-1]], lambda q, j=j: q.tensor_tensor(
                out=b_n2f[cW(j)], in0=b_x2[cW(j)], in1=pBC[:, 0:WA], op=ALU.mult))
            n2b = p.s(V, [n2a], lambda q, j=j: q.tensor_tensor(
                out=b_n2f[cW(j)], in0=b_n2f[cW(j)], in1=pBC[:, WA:2 * WA],
                op=ALU.subtract))
            n2c = p.s(V, [n2b], lambda q, j=j: q.tensor_scalar(
                out=b_n2f[cW(j)], in0=b_n2f[cW(j)], scalar1=g_g2[:, j:j + 1],
                scalar2=g_be2[:, j:j + 1], op0=ALU.mult, op1=ALU.add))
            n2j = p.s(Sc, [n2c], lambda q, j=j, t0=t0: q.copy(
                out=n2bf[:, j * TOK + t0:j * TOK + t0 + WA], in_=b_n2f[cW(j)]))
            n2s.append(n2j)
        pBC_free = n2s[-1]
        n2_last = n2s[-1]
        rz = p.s(T, [n2s[-1], pT_free], lambda q: [
            q.matmul(pT_[0:E, 0:WA], g_rw[:, k * E:(k + 1) * E], b_n2f[cW(k)],
                     start=(k == 0), stop=(k == DK - 1)) for k in range(DK)][-1])
        rze = p.s(V, [rz], lambda q: q.tensor_copy(out=b_zr[:], in_=pT_[0:E, 0:WA]))
        ztr = p.s(T, [rze], lambda q: [
            q.transpose(pT_[:, 256 + s * E:256 + (s + 1) * E],
                        b_zr[:, s * 128:(s + 1) * 128], g_ident[0:E, 0:E])
            for s in range(2)][-1])
        zev = p.s(V, [ztr], lambda q: q.tensor_copy(
            out=b_ztm[:], in_=pT_[:, 256:256 + 2 * E]))
        pT_free = zev
        prev_cmb = None
        x12s = [None] * DK
        for sblk in range(2):
            blk = 2 * i + sblk
            zs = (slice(None), slice(sblk * E, (sblk + 1) * E))
            mx = p.s(V, [zev], lambda q, zs=zs: q.max(out=b_srt[zs], in_=b_ztm[zs]))
            nm = p.s(V, [mx], lambda q, sblk=sblk: q.tensor_scalar(
                out=b_nm[:, 2 * sblk:2 * sblk + 2],
                in0=b_srt[:, sblk * E:sblk * E + 2], scalar1=-1.0, scalar2=None,
                op0=ALU.mult))
            ex = p.s(Sc, [nm], lambda q, zs=zs, sblk=sblk: q.activation(
                out=b_ez[zs], in_=b_ztm[zs], func=AF.Exp,
                bias=b_nm[:, 2 * sblk:2 * sblk + 1], scale=1.0))
            sg = p.s(Sc, [ex, prev_cmb], lambda q, zs=zs, sblk=sblk: q.sign(
                out=b_sgn[:], in_=b_ztm[zs],
                bias=b_nm[:, 2 * sblk + 1:2 * sblk + 2]))
            c0 = p.s(V, [sg], lambda q, zs=zs, sblk=sblk: q.reduce_sum(
                b_nm[:, 2 * sblk:2 * sblk + 1], b_ez[zs], axis=AXX))
            c1 = p.s(V, [c0], lambda q: q.tensor_scalar(
                out=b_sgn[:], in0=b_sgn[:], scalar1=1.0, scalar2=1.0,
                op0=ALU.add, op1=ALU.min))
            c2 = p.s(V, [c1], lambda q, zs=zs: q.tensor_tensor(
                out=b_ez[zs], in0=b_ez[zs], in1=b_sgn[:], op=ALU.mult))
            c3 = p.s(V, [c2], lambda q, sblk=sblk: q.reciprocal(
                out=b_nm[:, 2 * sblk:2 * sblk + 1],
                in_=b_nm[:, 2 * sblk:2 * sblk + 1]))
            cmb = p.s(V, [c3], lambda q, zs=zs, blk=blk, sblk=sblk: q.tensor_scalar(
                out=g_combine[:, blk * E:(blk + 1) * E], in0=b_ez[zs],
                scalar1=b_nm[:, 2 * sblk:2 * sblk + 1], scalar2=None,
                op0=ALU.mult))
            prev_cmb = cmb
            comb_last = cmb
            ct = p.s(T, [cmb, pT_free], lambda q, blk=blk: q.transpose(
                pT_[0:E, 256:384], g_combine[:, blk * E:(blk + 1) * E],
                g_ident[:]))
            ce = p.s(V, [ct], lambda q, sblk=sblk: q.tensor_copy(
                out=b_cebf[:, sblk * 128:(sblk + 1) * 128], in_=pT_[0:E, 256:384]))
            pT_free = ce
            xts = []
            xt_prev = None
            for j in range(DK):
                if sblk == 0:
                    x12s[j] = p.s(V, [x2s[j]], lambda q, j=j: q.tensor_tensor(
                        out=b_n1[cW(j)], in0=b_x1[cW(j)], in1=b_x2[cW(j)],
                        op=ALU.add))
                xt = p.s(T, [x12s[j], pT_free, xt_prev],
                         lambda q, j=j, sblk=sblk: q.transpose(
                             pT_[:, 128:256],
                             b_n1[:, j * WA + sblk * 128:j * WA + (sblk + 1) * 128],
                             g_ident[:]))
                xe = p.s(V, [xt], lambda q, j=j, blk=blk: q.tensor_copy(
                    out=out_acc[:, blk * D + j * 128:blk * D + (j + 1) * 128],
                    in_=pT_[:, 128:256]))
                xt_prev = xe
                xts.append(xe)
            pT_free = xts[-1]
            b2m = p.s(T, [ce, ldb2, pB2_free], lambda q, sblk=sblk: [
                q.matmul(pB2[:, hh * 512:(hh + 1) * 512],
                         b_cebf[:, sblk * 128:(sblk + 1) * 128],
                         g_b2bf[:, hh * 512:(hh + 1) * 512],
                         start=True, stop=True) for hh in range(2)][-1])
            b2e = p.s(V, [b2m, xts[-1]], lambda q, blk=blk: q.tensor_tensor(
                out=out_acc[:, blk * D:(blk + 1) * D],
                in0=out_acc[:, blk * D:(blk + 1) * D], in1=pB2[:, 0:1024],
                op=ALU.add))
            pB2_free = b2e
            oacc_last = b2e
        ap_prev = [x1s[-1], apbf, apsq]

    if "N2" in tap_outs:
        p.s(Sy, [n2_last], lambda q: [
            q.dma_start(out=tap_outs["N2"], in_=n2bf[:])], dma=1, chan="c_tap")
    if "CMB" in tap_outs:
        p.s(Sy, [comb_last], lambda q: [
            q.dma_start(out=tap_outs["CMB"], in_=g_combine[:])], dma=1, chan="c_tap")

    snapB1 = p.snap()
    barrierB1 = p.all_deps(snapB1)
    ctxBp.close()
    ctxB1.close()
    if upto == "B1":
        return finish_dummy(None)

    hbuf = sbB("hbuf", (128, 16 * TOK), BF16)
    w2fb = sbB("w2fb", (128, 16 * D), BF16)
    ctxM1 = ExitStack()
    w1t = [ctxM1.enter_context(nc.sbuf_tensor(f"w1t{i}", [128, DK * 2 * 128],
                                              BF16)) for i in range(2)]
    ctxMp = ExitStack()
    p2 = [ctxMp.enter_context(nc.psum_tensor(f"p2_{i}", [128, 1024], F32))
          for i in range(4)]

    E_range = [0] if upto in ("moe0", "p1", "p2nm") else range(E)
    FB_range = [0] if upto in ("moe0", "p1", "p2nm") else range(2)
    w1_free = [barrierB1, barrierB1]
    w2_prev = barrierB1
    pass1_bank_free = [barrierB1, barrierB1]
    pass2_last_evac = barrierB1
    hbuf_free = barrierB1
    for e in E_range:
        for fb in FB_range:
            w2l = None
            silu_last = None
            for fcg in range(8):
                wsel = fcg % 2
                w1l = p.s(G, [w1_free[wsel]], lambda q, e=e, fb=fb, fcg=fcg,
                          wsel=wsel: [
                    q.dma_start(
                        out=w1t[wsel][:].rearrange("p (k l f) -> p k l f",
                                                   k=DK, l=2),
                        in_=w1h[e, fb * 8 + fcg])], dma=1, chan=f"c_w1{wsel}")
                if fcg == 2:
                    w2l = p.s(G, [w2_prev, pass2_last_evac],
                              lambda q, e=e, fb=fb: [
                        q.dma_start(out=w2fb[:].rearrange("p (fc d) -> p fc d",
                                                          fc=16),
                                    in_=w2h[e, fb])], dma=1, chan="c_w2")
                    w2_prev = w2l
                for fcl in range(2):
                    fcg_l = 2 * fcg + fcl
                    fc_g = fb * 16 + fcg_l
                    for ts in range(4):
                        bk = fcg_l % 2
                        mm = p.s(T, [w1l, pass1_bank_free[bk], n2_last],
                                 lambda q, wsel=wsel, fcl=fcl, ts=ts, bk=bk: [
                            q.matmul(p2[0][:, bk * 512:bk * 512 + 512],
                                     w1t[wsel][:, k * 256 + fcl * 128:
                                               k * 256 + fcl * 128 + 128],
                                     n2bf[:, k * TOK + ts * 512:
                                          k * TOK + (ts + 1) * 512],
                                     start=(k == 0), stop=(k == DK - 1))
                            for k in range(DK)][-1])
                        sl = p.s(Sc, [mm, hbuf_free],
                                 lambda q, e=e, fc_g=fc_g, fcg_l=fcg_l, ts=ts,
                                 bk=bk: q.activation(
                                     out=hbuf[:, fcg_l * TOK + ts * 512:
                                              fcg_l * TOK + (ts + 1) * 512],
                                     in_=p2[0][:, bk * 512:bk * 512 + 512],
                                     func=AF.Silu,
                                     bias=g_b1[:, e * FC + fc_g:e * FC + fc_g + 1],
                                     scale=1.0))
                        pass1_bank_free[bk] = sl
                        silu_last = sl
                w1_free[wsel] = silu_last
            if upto == "p1":
                pass2_last_evac = silu_last
                continue
            new_evac = []
            prev_ev = [pass2_last_evac] * 4
            for rnd in range(4):
                for tb in range(4):
                    tbg = rnd * 4 + tb
                    mmtb = p.s(T, [silu_last, w2l, prev_ev[tb]],
                               lambda q, rnd=rnd, tb=tb: [
                        q.matmul(p2[tb][:, dc * 512:(dc + 1) * 512],
                                 hbuf[:, fcl * TOK + rnd * 512 + tb * 128:
                                      fcl * TOK + rnd * 512 + (tb + 1) * 128],
                                 w2fb[:, fcl * D + dc * 512:
                                      fcl * D + (dc + 1) * 512],
                                 start=(fcl == 0), stop=(fcl == 15))
                        for fcl in range(16) for dc in range(2)][-1])
                    ev = p.s(V, [mmtb, comb_last, oacc_last],
                             lambda q, tb=tb, tbg=tbg, e=e: q.scalar_tensor_tensor(
                                 out=out_acc[:, tbg * D:(tbg + 1) * D],
                                 in0=p2[tb][:, 0:1024],
                                 scalar=g_combine[:, tbg * E + e:tbg * E + e + 1],
                                 in1=out_acc[:, tbg * D:(tbg + 1) * D],
                                 op0=ALU.mult, op1=ALU.add))
                    prev_ev[tb] = ev
                    new_evac.append(ev)
            pass2_last_evac = new_evac[-1]
            pass1_bank_free = [new_evac[-1], new_evac[-1]]
            hbuf_free = new_evac[-1]
            if e == E - 1 and fb == 1:
                final_evacs = new_evac
    moe_done = pass2_last_evac
    if upto in ("moe0", "moe", "p1", "p2nm"):
        ctxM1.close()
        return finish_dummy(moe_done, (ctxMp,))

    if "OACC" in tap_outs:
        p.s(Sy, [moe_done], lambda q: [
            q.dma_start(out=tap_outs["OACC"], in_=out_acc[:])], dma=1, chan="c_tap")

    ctxM1.close()
    f_sq = sbB("f_sq", (128, D))
    f_row = sbB("f_row", (128, 8))
    f_out = sbB("f_out", (128, D))
    out_deps = []
    prev_rd = None
    prev_od = None
    for blk in range(TOK // 128):
        bs = (slice(None), slice(blk * D, (blk + 1) * D))
        sq = p.s(Sc, [final_evacs[blk], prev_rd], lambda q, bs=bs: q.activation(
            out=f_sq[:], in_=out_acc[bs], func=AF.Square))
        rd0 = p.s(V, [sq, prev_od], lambda q, bs=bs: (
            q.reduce_sum(f_row[:, 0:1], out_acc[bs], axis=AXX),
            q.reduce_sum(f_row[:, 1:2], f_sq[:], axis=AXX))[-1])
        rd1 = p.s(V, [rd0], lambda q: q.tensor_scalar(
            out=f_row[:, 0:2], in0=f_row[:, 0:2], scalar1=1.0 / D,
            scalar2=None, op0=ALU.mult))
        rd2 = p.s(V, [rd1], lambda q: q.tensor_tensor(
            out=f_row[:, 2:3], in0=f_row[:, 0:1], in1=f_row[:, 0:1],
            op=ALU.mult))
        rd = p.s(V, [rd2], lambda q: q.tensor_tensor(
            out=f_row[:, 2:3], in0=f_row[:, 1:2], in1=f_row[:, 2:3],
            op=ALU.subtract))
        rs0 = p.s(Sc, [rd], lambda q: q.activation(
            out=f_row[:, 3:4], in_=f_row[:, 2:3], func=AF.Ln,
            bias=g_cp[:, 1:2], scale=1.0))
        rs = p.s(Sc, [rs0], lambda q: q.activation(
            out=f_row[:, 3:4], in_=f_row[:, 3:4], func=AF.Exp, scale=-0.5))
        fo = p.s(V, [rs, prev_od], lambda q, bs=bs: q.tensor_scalar(
            out=f_out[:], in0=out_acc[bs], scalar1=f_row[:, 0:1],
            scalar2=f_row[:, 3:4], op0=ALU.subtract, op1=ALU.mult))
        od = p.s(Sy, [fo], lambda q, blk=blk: [
            q.dma_start(out=OUT[blk * 128:(blk + 1) * 128, :], in_=f_out[:])],
            dma=1, chan="c_out")
        prev_rd = rd
        prev_od = od
        out_deps.append(od)

    tapdep = [("c_tap", p.counts["c_tap"])] if "c_tap" in p.counts else []
    p.s(Sy, [out_deps[-1]] + tapdep, lambda q: q.nop())

    block = ctx.enter_context(nc.Block())
    p.emit(block)
    ctxMp.close()
    ctxB.close()
    ctx.close()
    return nc


def _host_inputs(inputs):
    x = np.asarray(inputs["L0_attn_x"], np.float32)
    pos = np.arange(S, dtype=np.float64)[:, None]
    div = np.exp(np.arange(0, PE_DIM, 2, dtype=np.float64) *
                 (-np.log(10000.0) / PE_DIM))
    pe = np.zeros((S, PE_DIM), np.float64)
    pe[:, 0::2] = np.sin(pos * div)
    pe[:, 1::2] = np.cos(pos * div)
    pe = (pe / 100.0).astype(np.float32)

    v8 = lambda v: np.ascontiguousarray(
        np.asarray(v, np.float32).reshape(DK, 128).T)
    Wdx = np.asarray(inputs["L0_attn_W_dx"], np.float32)
    Wx = np.asarray(inputs["L0_attn_W_x"], np.float32)
    w1 = np.asarray(inputs["L0_ffn_w1"], np.float32)
    b1 = np.asarray(inputs["L0_ffn_b1"], np.float32)
    w1hh = np.ascontiguousarray(
        w1.reshape(E, DK, 128, 16, 2, 128).transpose(0, 3, 2, 1, 4, 5))
    b1t = np.ascontiguousarray(b1.reshape(E, FC, 128).transpose(2, 0, 1)
                               .reshape(128, E * FC))

    shared = dict(
        Wdx_m=np.ascontiguousarray(Wdx[:D]),
        Wdx_p=np.ascontiguousarray(Wdx[D:]),
        Wx_m=np.ascontiguousarray(Wx[:D]),
        Wx_p=np.ascontiguousarray(Wx[D:]),
        Wgx=np.asarray(inputs["L0_attn_W_gx"], np.float32),
        Wga=np.asarray(inputs["L0_attn_W_ga"], np.float32),
        Wo=np.asarray(inputs["L0_attn_W_o"], np.float32),
        mixw=np.asarray(inputs["L0_ffn_mix_w"], np.float32),
        rw=np.asarray(inputs["L0_ffn_router_w"], np.float32),
        bdx=v8(inputs["L0_attn_b_dx"]), bx=v8(inputs["L0_attn_b_x"]),
        bgx=v8(inputs["L0_attn_b_gx"]), bga=v8(inputs["L0_attn_b_ga"]),
        bo=v8(inputs["L0_attn_b_o"]), mixb=v8(inputs["L0_ffn_mix_b"]),
        alog=v8(inputs["L0_attn_a_logit"]),
        g1=v8(inputs["L0_ffn_g1"]), be1=v8(inputs["L0_ffn_be1"]),
        g2=v8(inputs["L0_ffn_g2"]), be2=v8(inputs["L0_ffn_be2"]),
        b1h=b1t, b2h=np.asarray(inputs["L0_ffn_b2"], np.float32),
        w1h=w1hh,
        w2h=np.ascontiguousarray(
            np.asarray(inputs["L0_ffn_w2"], np.float32)
            .reshape(E, 2, 16, 128, D).transpose(0, 1, 3, 2, 4)),
        ident=np.eye(128, dtype=np.float32),
    )
    in_maps = []
    for c in range(8):
        b, h = c // 2, c % 2
        d = dict(shared)
        d["xT"] = np.ascontiguousarray(x[b, h * TOK:(h + 1) * TOK, :].T)
        d["peT"] = np.ascontiguousarray(pe[h * TOK:(h + 1) * TOK, :].T)
        d["me"] = np.full((128, 1), 1.0 if h == 0 else 0.0, np.float32)
        d["mo"] = np.full((128, 1), 1.0 if h == 1 else 0.0, np.float32)
        in_maps.append(d)
    return in_maps


def kernel(**inputs):
    in_maps = _host_inputs(inputs)
    nc = build_program()
    res = run_bass_kernel_spmd(nc, in_maps, core_ids=list(range(8)))
    out = np.empty((B, S, D), np.float32)
    for c in range(8):
        b, h = c // 2, c % 2
        out[b, h * TOK:(h + 1) * TOK, :] = res.results[c]["OUT"]
    return out


# version 19 (final)
# speedup vs baseline: 1.1856x; 1.1856x over previous
import numpy as np
from contextlib import ExitStack

import concourse.bass as bass
import concourse.mybir as mybir
from concourse.bass_utils import run_bass_kernel_spmd

F32 = mybir.dt.float32
BF16 = mybir.dt.bfloat16
AF = mybir.ActivationFunctionType
ALU = mybir.AluOpType
AXX = mybir.AxisListType.X

ENGS = ("sync", "tensor", "vector", "scalar", "gpsimd")

B, S, D, DK = 4, 4096, 1024, 8
TOK = 2048
PE_DIM = 16
F, FC = 4096, 32
E = 8
WA = 256
NTA = TOK // WA
EPS = 1e-5


class Prog:

    def __init__(self, nc, ctx):
        self.nc = nc
        self.ctx = ctx
        self.sems = {}
        self.counts = {}
        self.queues = {e: [] for e in ENGS}
        self.last_waited = {}
        for e in ENGS:
            self._chan(e)

    def _chan(self, name):
        if name not in self.sems:
            self.sems[name] = self.ctx.enter_context(
                self.nc.semaphore("sem_" + name))
            self.counts[name] = 0
        return name

    def s(self, eng, deps, fn, dma=0, chan=None):
        chan = self._chan(chan or eng)
        waits = []
        flat = []
        for d in deps:
            if not d:
                continue
            if isinstance(d, tuple) and len(d) == 2 and isinstance(d[0], str):
                flat.append(d)
            else:
                flat.extend(x for x in d if x)
        for te, tc in flat:
            if tc <= 0:
                continue
            key = (eng, te)
            if tc <= self.last_waited.get(key, 0):
                continue
            self.last_waited[key] = tc
            waits.append((self.sems[te], tc))
        inc = 16 * dma if dma else 1
        sem = self.sems[chan]
        self.queues[eng].append((waits, fn, sem, inc, dma))
        self.counts[chan] += inc
        return (chan, self.counts[chan])

    def snap(self):
        return dict(self.counts)

    def all_deps(self, snapshot=None):
        snapshot = snapshot or self.counts
        return [(e, v) for e, v in snapshot.items()]

    def emit(self, block):
        def make(engname):
            def body(q):
                for waits, fn, sem, inc, dma in self.queues[engname]:
                    for s, v in waits:
                        q.wait_ge(s, v)
                    res = fn(q)
                    if dma:
                        assert isinstance(res, (list, tuple)) and len(res) == dma
                        for instr in res:
                            instr.then_inc(sem, 16)
                    else:
                        res.then_inc(sem, 1)
            return body

        block.sync(make("sync"))
        block.tensor(make("tensor"))
        block.vector(make("vector"))
        block.scalar(make("scalar"))
        block.gpsimd(make("gpsimd"))


def build_program(taps=(), upto="full"):
    nc = bass.Bass("TRN2", target_bir_lowering=False, debug=False)

    def inp(name, shape, dtype=F32):
        return nc.dram_tensor(name, list(shape), dtype, kind="ExternalInput").ap()

    xT = inp("xT", (D, TOK))
    peT = inp("peT", (PE_DIM, TOK))
    Wdx_m = inp("Wdx_m", (D, D))
    Wdx_p = inp("Wdx_p", (PE_DIM, D))
    Wx_m = inp("Wx_m", (D, D))
    Wx_p = inp("Wx_p", (PE_DIM, D))
    Wgx = inp("Wgx", (D, D))
    Wga = inp("Wga", (D, D))
    Wo = inp("Wo", (D, D))
    mixw = inp("mixw", (D, D))
    rw = inp("rw", (D, E))
    bdx = inp("bdx", (128, DK))
    bx = inp("bx", (128, DK))
    bgx = inp("bgx", (128, DK))
    bga = inp("bga", (128, DK))
    bo = inp("bo", (128, DK))
    mixb = inp("mixb", (128, DK))
    alog = inp("alog", (128, DK))
    g1 = inp("g1", (128, DK))
    be1 = inp("be1", (128, DK))
    g2 = inp("g2", (128, DK))
    be2 = inp("be2", (128, DK))
    b1h = inp("b1h", (128, E * FC))
    b2h = inp("b2h", (E, D))
    w1h = inp("w1h", (E, 16, 128, DK, 2, 128))
    w2h = inp("w2h", (E, 2, 128, 16, D))
    me = inp("me", (128, 1))
    mo = inp("mo", (128, 1))
    ident = inp("ident", (128, 128))

    OUT = nc.dram_tensor("OUT", [TOK, D], F32, kind="ExternalOutput").ap()
    APSP = nc.dram_tensor("APSP", [128, DK * TOK], F32, kind="Internal").ap()
    ACSP = nc.dram_tensor("ACSP", [128, DK * TOK], BF16, kind="Internal").ap()
    CRSP = nc.dram_tensor("CRSP", [128, DK * TOK], F32, kind="Internal").ap()
    CIN = nc.dram_tensor("CIN", [1, D], F32, kind="Internal").ap()
    COUT = nc.dram_tensor("COUT", [1, D], F32, kind="Internal").ap()

    tap_outs = {}
    for t in taps:
        shp, dt = {"APSP": ([128, DK * TOK], F32), "CRSP": ([128, DK * TOK], F32),
                   "N2": ([128, DK * TOK], BF16), "CMB": ([128, 16 * E], F32),
                   "OACC": ([128, 16 * D], F32),
                   "XH": ([128, DK * TOK], F32), "AA": ([128, DK * TOK], F32),
                   "CC": ([128, DK * TOK], F32), "HS": ([128, DK * TOK], BF16),
                   }[t]
        tap_outs[t] = nc.dram_tensor("DBG_" + t, shp, dt, kind="ExternalOutput").ap()

    ctx = ExitStack()
    sb = lambda name, shape, dtype=F32: ctx.enter_context(
        nc.sbuf_tensor(name, list(shape), dtype))

    g_bdx = sb("g_bdx", (128, DK)); g_bx = sb("g_bx", (128, DK))
    g_bgx = sb("g_bgx", (128, DK)); g_bga = sb("g_bga", (128, DK))
    g_bo = sb("g_bo", (128, DK)); g_mixb = sb("g_mixb", (128, DK))
    g_alog = sb("g_alog", (128, DK)); g_g1 = sb("g_g1", (128, DK))
    g_be1 = sb("g_be1", (128, DK)); g_g2 = sb("g_g2", (128, DK))
    g_be2 = sb("g_be2", (128, DK))
    g_ae = sb("g_ae", (128, DK)); g_saq = sb("g_saq", (128, DK))
    g_bhalf = sb("g_bhalf", (128, DK))
    g_bsaq = sb("g_bsaq", (128, DK))
    g_b1 = sb("g_b1", (128, E * FC))
    g_b2bf = sb("g_b2bf", (E, D), BF16)
    g_rw = sb("g_rw", (128, DK * E))
    g_me = sb("g_me", (128, 1)); g_mo = sb("g_mo", (128, 1))
    g_ident = sb("g_ident", (128, 128))
    g_H = sb("g_H", (128, DK))
    g_hc = sb("g_hc", (128, DK))
    g_ar = sb("g_ar", (128, DK))
    g_cp = sb("g_cp", (128, 4))
    g_ones1 = sb("g_ones1", (1, 128))
    g_onescol = sb("g_onescol", (128, 1), BF16)
    g_onesW = sb("g_onesW", (128, WA))

    p = Prog(nc, ctx)
    V, Sc, T, Sy, G = "vector", "scalar", "tensor", "sync", "gpsimd"

    smalls = [(g_bdx, bdx), (g_bx, bx), (g_bgx, bgx), (g_bga, bga), (g_bo, bo),
              (g_mixb, mixb), (g_alog, alog), (g_g1, g1), (g_be1, be1),
              (g_g2, g2), (g_be2, be2), (g_b1, b1h), (g_me, me), (g_mo, mo),
              (g_ident, ident), (g_rw.rearrange("p (k e) -> p k e", k=DK), rw.rearrange("(k p) e -> p k e", p=128))]
    def _ap(x):
        return x if isinstance(x, bass.AP) else x[:]
    ld0 = p.s(Sy, [], lambda q, s=smalls: [
        q.dma_start(out=_ap(dst), in_=src) for dst, src in s], dma=len(smalls), chan="c_ld0")
    ldb2 = p.s(G, [], lambda q: [q.dma_start(out=g_b2bf[:], in_=b2h)], dma=1, chan="c_b2")

    cst = p.s(V, [], lambda q: (
        q.memset(g_cp[:, 0:1], 1.0),
        q.memset(g_cp[:, 1:2], EPS),
        q.memset(g_ones1[:], 1.0),
        q.memset(g_onescol[:], 1.0),
        q.memset(g_onesW[:], 1.0),
        q.memset(g_hc[:], 0.0),
        q.memset(g_ar[:], 1.0))[-1])
    _cut0 = (upto == "ldA")
    ae_s = p.s(Sc, [ld0], lambda q: q.activation(
        out=g_ae[:], in_=g_alog[:], func=AF.Sigmoid))
    sq_s = p.s(Sc, [ae_s], lambda q: q.activation(
        out=g_saq[:], in_=g_ae[:], func=AF.Square))
    saq_s = p.s(Sc, [sq_s, cst], lambda q: q.activation(
        out=g_saq[:], in_=g_saq[:], func=AF.Sqrt, scale=-1.0, bias=g_cp[:, 0:1]))
    bh_v0 = p.s(V, [ld0, cst], lambda q: q.tensor_tensor(
        out=g_bhalf[:], in0=g_bdx[:], in1=g_bo[:], op=ALU.add))
    bh_v = p.s(V, [bh_v0], lambda q: q.tensor_scalar(
        out=g_bhalf[:], in0=g_bhalf[:], scalar1=0.5, scalar2=None, op0=ALU.mult))
    bsaq_v = p.s(V, [saq_s, bh_v], lambda q: q.tensor_tensor(
        out=g_bsaq[:], in0=g_bx[:], in1=g_saq[:], op=ALU.mult))

    _cut1 = (upto == "ldB")
    ctxA = ExitStack()
    sbA = lambda name, shape, dtype=F32: ctxA.enter_context(
        nc.sbuf_tensor(name, list(shape), dtype))
    psA = lambda name: ctxA.enter_context(nc.psum_tensor(name, [128, 512], F32))

    a_wdx = sbA("a_wdx", (128, DK * D), BF16)
    a_wx = sbA("a_wx", (128, DK * D), BF16)
    a_wdxp = sbA("a_wdxp", (PE_DIM, D), BF16)
    a_wxp = sbA("a_wxp", (PE_DIM, D), BF16)
    a_wgx = sbA("a_wgx", (128, DK * D), BF16)
    a_wga = sbA("a_wga", (128, DK * D), BF16)
    a_wo = sbA("a_wo", (128, DK * D), BF16)
    a_pebf = sbA("a_pebf", (PE_DIM, TOK), BF16)
    a_xf = [sbA(f"a_xf{i}", (128, DK * WA)) for i in range(2)]
    a_xbf = [sbA(f"a_xbf{i}", (128, DK * WA), BF16) for i in range(2)]
    a_xhf = sbA("a_xhf", (128, DK * WA))
    a_xhbf = sbA("a_xhbf", (128, DK * WA), BF16)
    a_a = sbA("a_a", (128, DK * WA))
    a_xg = sbA("a_xg", (128, DK * WA))
    a_c = sbA("a_c", (128, DK * WA))
    a_hbf = [sbA(f"a_hbf{i}", (128, DK * WA), BF16) for i in range(2)]
    a_acbf = [sbA(f"a_acbf{i}", (128, DK * WA), BF16) for i in range(2)]
    a_apre = sbA("a_apre", (128, DK * WA))
    a_gt = sbA("a_gt", (128, 2 * WA))
    a_st = sbA("a_st", (128, WA))
    a_scf = sbA("a_scf", (128, 2 * WA))
    a_hcm = sbA("a_hcm", (128, DK))
    a_corrbf = sbA("a_corrbf", (128, DK * WA), BF16)
    a_corrf = sbA("a_corrf", (128, DK * WA))

    pXH = [psA("pXH0"), psA("pXH1")]
    pGA = [psA("pGA0"), psA("pGA1")]
    pGX = [psA("pGX0"), psA("pGX1")]
    pDW = [psA("pDW0"), psA("pDW1")]

    if upto in ("ldA", "ldB"):
        wload = ldb2
    else:
        wload = p.s(G, [ldb2], lambda q: [
            q.dma_start(out=a_wdx[:].rearrange("p (k n) -> p k n", k=DK), in_=Wdx_m.rearrange("(k p) n -> p k n", p=128)),
            q.dma_start(out=a_wx[:].rearrange("p (k n) -> p k n", k=DK), in_=Wx_m.rearrange("(k p) n -> p k n", p=128)),
            q.dma_start(out=a_wdxp[:], in_=Wdx_p),
            q.dma_start(out=a_wxp[:], in_=Wx_p),
            q.dma_start(out=a_wgx[:].rearrange("p (k n) -> p k n", k=DK), in_=Wgx.rearrange("(k p) n -> p k n", p=128)),
            q.dma_start(out=a_wga[:].rearrange("p (k n) -> p k n", k=DK), in_=Wga.rearrange("(k p) n -> p k n", p=128)),
            q.dma_start(out=a_wo[:].rearrange("p (k n) -> p k n", k=DK), in_=Wo.rearrange("(k p) n -> p k n", p=128)),
            q.dma_start(out=a_pebf[:], in_=peT),
        ], dma=8, chan="c_wl")

    xT3 = xT.rearrange("(k p) t -> p k t", p=128)
    AP3 = APSP.rearrange("p (k t) -> p k t", k=DK)
    AC3 = ACSP.rearrange("p (k t) -> p k t", k=DK)
    CR3 = CRSP.rearrange("p (k t) -> p k t", k=DK)

    xf_free = [None, None]
    xh_bank_free = [None, None]
    last_dxwo_mm = None
    last_gate_mm = None
    last_vx = None
    ga_bank_free = [None, None]
    gx_bank_free = [None, None]
    dw_bank_free = [None, None]
    spill_prev = None
    scan_done = []

    cW = lambda j: (slice(None), slice(j * WA, (j + 1) * WA))

    fstates = []

    def front(i):
        nonlocal_state = {}
        t0 = i * WA
        bsel = i % 2
        xf = a_xf[bsel]
        xbf = a_xbf[bsel]
        hbf = a_hbf[bsel]
        acbf = a_acbf[bsel]
        ldx = p.s(Sy, [st_.get("xf_free%d" % bsel)], lambda q, t0=t0, xf=xf: [
            q.dma_start(out=xf[:].rearrange("p (k t) -> p k t", k=DK),
                        in_=xT3[:, :, t0:t0 + WA])], dma=1, chan=f"c_x{bsel}")
        cast_x = p.s(Sc, [ldx, st_.get("last_dxwo_mm")],
                     lambda q, xf=xf, xbf=xbf: q.copy(out=xbf[:], in_=xf[:]))
        cast_x_dep = [cast_x, wload] if i == 0 else [cast_x]

        evac_xh = []
        for j in range(DK):
            bk = j % 2
            mm = p.s(T, cast_x_dep + [st_.get("xh_bank_free%d" % bk)],
                     lambda q, j=j, bk=bk, t0=t0, xbf=xbf: (
                         [q.matmul(pXH[bk][:, 0:WA],
                                   a_wx[:, k * D + j * 128:k * D + j * 128 + 128],
                                   xbf[cW(k)], start=(k == 0), stop=False)
                          for k in range(DK)],
                         q.matmul(pXH[bk][:, 0:WA], a_wxp[:, j * 128:(j + 1) * 128],
                                  a_pebf[:, t0:t0 + WA], start=False,
                                  stop=True))[-1])
            ev = p.s(Sc, [mm, bsaq_v, st_.get("last_vx")],
                     lambda q, j=j, bk=bk: q.activation(
                         out=a_xhf[cW(j)], in_=pXH[bk][:, 0:WA], func=AF.Identity,
                         scale=g_saq[:, j:j + 1], bias=g_bsaq[:, j:j + 1]))
            st_["xh_bank_free%d" % bk] = ev
            evac_xh.append(ev)
        cast_xh = p.s(G, [evac_xh[-1], st_.get("last_gate_mm")],
                      lambda q: q.tensor_copy(out=a_xhbf[:], in_=a_xhf[:]))

        prev_va, prev_vx = None, None
        axg_steps = []
        for j in range(DK):
            bk = j % 2
            mma = p.s(T, [cast_xh, st_.get("ga_bank_free%d" % bk)],
                      lambda q, j=j, bk=bk: [
                q.matmul(pGA[bk][:, 0:WA],
                         a_wga[:, k * D + j * 128:k * D + j * 128 + 128],
                         a_xhbf[cW(k)], start=(k == 0), stop=(k == DK - 1))
                for k in range(DK)][-1])
            mmx = p.s(T, [cast_xh, st_.get("gx_bank_free%d" % bk)],
                      lambda q, j=j, bk=bk: [
                q.matmul(pGX[bk][:, 0:WA],
                         a_wgx[:, k * D + j * 128:k * D + j * 128 + 128],
                         a_xhbf[cW(k)], start=(k == 0), stop=(k == DK - 1))
                for k in range(DK)][-1])
            sa = p.s(Sc, [mma, prev_va], lambda q, bk=bk, j=j: q.activation(
                out=a_gt[:, 0:WA], in_=pGA[bk][:, 0:WA], func=AF.Sigmoid,
                bias=g_bga[:, j:j + 1], scale=1.0))
            sx = p.s(Sc, [mmx, prev_vx], lambda q, bk=bk, j=j: q.activation(
                out=a_gt[:, WA:2 * WA], in_=pGX[bk][:, 0:WA], func=AF.Sigmoid,
                bias=g_bgx[:, j:j + 1], scale=1.0))
            va = p.s(V, [sa, ae_s], lambda q, j=j: q.tensor_scalar(
                out=a_a[cW(j)], in0=a_gt[:, 0:WA], scalar1=g_ae[:, j:j + 1],
                scalar2=None, op0=ALU.mult))
            vx = p.s(V, [sx], lambda q, j=j: q.tensor_tensor(
                out=a_xg[cW(j)], in0=a_gt[:, WA:2 * WA], in1=a_xhf[cW(j)],
                op=ALU.mult))
            st_["ga_bank_free%d" % bk] = sa
            st_["gx_bank_free%d" % bk] = sx
            prev_va, prev_vx = va, vx
            st_["last_gate_mm"] = mmx
            axg_steps.append((va, vx))
        st_["last_vx"] = prev_vx

        c_steps = []
        prev_cv = None
        for j in range(DK):
            va, vx = axg_steps[j]
            s1 = p.s(Sc, [va, prev_cv], lambda q, j=j: q.activation(
                out=a_st[:], in_=a_a[cW(j)], func=AF.Square))
            s2 = p.s(Sc, [s1], lambda q: q.activation(
                out=a_st[:], in_=a_st[:], func=AF.Sqrt, scale=-1.0,
                bias=g_cp[:, 0:1]))
            cv = p.s(V, [s2, vx], lambda q, j=j: q.tensor_tensor(
                out=a_c[cW(j)], in0=a_st[:], in1=a_xg[cW(j)], op=ALU.mult))
            c_steps.append(cv)
            prev_cv = cv

        scan_done = []
        prev_upc = st_.get("prev_upc")
        prev_uph = st_.get("prev_uph")
        prev_upa = st_.get("prev_upa")
        for j in range(DK):
            sc1 = p.s(V, [c_steps[j], prev_upc, prev_uph],
                      lambda q, j=j: q.tensor_tensor_scan(
                          out=a_scf[:, 0:WA], data0=a_a[cW(j)], data1=a_c[cW(j)],
                          initial=g_hc[:, j:j + 1], op0=ALU.mult, op1=ALU.add))
            sc2 = p.s(V, [sc1, prev_upa], lambda q, j=j: q.tensor_tensor_scan(
                out=a_scf[:, WA:2 * WA], data0=a_a[cW(j)], data1=g_onesW[:],
                initial=g_ar[:, j:j + 1], op0=ALU.mult, op1=ALU.mult))
            upc = p.s(V, [sc1, sc2], lambda q, j=j: (
                q.tensor_copy(out=g_hc[:, j:j + 1], in_=a_scf[:, WA - 1:WA]),
                q.tensor_copy(out=g_ar[:, j:j + 1],
                              in_=a_scf[:, 2 * WA - 1:2 * WA]))[-1])
            uph = p.s(Sc, [sc1, st_.get("last_dxwo_mm")],
                      lambda q, j=j, hbf=hbf: q.copy(
                          out=hbf[cW(j)], in_=a_scf[:, 0:WA]))
            upa = p.s(G, [sc2, st_.get("spill_prev")],
                      lambda q, j=j, acbf=acbf: q.tensor_copy(
                          out=acbf[cW(j)], in_=a_scf[:, WA:2 * WA]))
            scan_done.append((upc, uph, upa))
            prev_upc, prev_uph, prev_upa = upc, uph, upa
        st_["prev_upc"] = prev_upc
        st_["prev_uph"] = prev_uph
        st_["prev_upa"] = prev_upa
        return dict(i=i, t0=t0, bsel=bsel, xf=xf, xbf=xbf, hbf=hbf, acbf=acbf,
                    scan_done=scan_done)

    def back(fs):
        i, t0, bsel = fs["i"], fs["t0"], fs["bsel"]
        xf, xbf, hbf, acbf = fs["xf"], fs["xbf"], fs["hbf"], fs["acbf"]
        scan_done = fs["scan_done"]
        apre_steps = []
        for j in range(DK):
            bk = j % 2
            mm = p.s(T, [scan_done[-1][1], st_.get("dw_bank_free%d" % bk)],
                     lambda q, j=j, bk=bk, t0=t0, xbf=xbf, hbf=hbf: (
                         [q.matmul(pDW[bk][:, 0:WA],
                                   a_wdx[:, k * D + j * 128:k * D + j * 128 + 128],
                                   xbf[cW(k)], start=(k == 0), stop=False)
                          for k in range(DK)],
                         q.matmul(pDW[bk][:, 0:WA],
                                  a_wdxp[:, j * 128:(j + 1) * 128],
                                  a_pebf[:, t0:t0 + WA], start=False, stop=False),
                         [q.matmul(pDW[bk][:, 0:WA],
                                   a_wo[:, k * D + j * 128:k * D + j * 128 + 128],
                                   hbf[cW(k)], start=False, stop=(k == DK - 1))
                          for k in range(DK)])[-1][-1])
            ev0 = p.s(Sc, [mm, st_.get("spill_prev")],
                      lambda q, j=j, bk=bk: q.activation(
                          out=a_apre[cW(j)], in_=pDW[bk][:, 0:WA],
                          func=AF.Identity, scale=0.5, bias=g_bhalf[:, j:j + 1]))
            ev = p.s(V, [ev0], lambda q, j=j, xf=xf: q.tensor_tensor(
                out=a_apre[cW(j)], in0=a_apre[cW(j)], in1=xf[cW(j)], op=ALU.add))
            st_["dw_bank_free%d" % bk] = ev
            st_["last_dxwo_mm"] = mm
            apre_steps.append(ev)
        st_["xf_free%d" % bsel] = apre_steps[-1]

        sp = p.s(Sy, [apre_steps[-1], scan_done[-1][2], scan_done[-1][1],
                      st_.get("spill_prev")],
                 lambda q, t0=t0, acbf=acbf: [
                     q.dma_start(out=AP3[:, :, t0:t0 + WA],
                                 in_=a_apre[:].rearrange("p (k t) -> p k t", k=DK)),
                     q.dma_start(out=AC3[:, :, t0:t0 + WA],
                                 in_=acbf[:].rearrange("p (k t) -> p k t", k=DK)),
                 ], dma=2, chan="c_sp")
        st_["spill_prev"] = sp

    st_ = {}
    for i in range(NTA):
        fstates.append(front(i))
        if i >= 1:
            back(fstates[i - 1])
    back(fstates[NTA - 1])
    spill_prev = st_["spill_prev"]
    scan_done = fstates[-1]["scan_done"]

    hcm_v = p.s(V, [spill_prev, scan_done[-1][0], ld0], lambda q: q.tensor_scalar(
        out=a_hcm[:], in0=g_hc[:], scalar1=g_me[:], scalar2=None, op0=ALU.mult))
    cin_d = p.s(Sy, [hcm_v], lambda q: [q.dma_start(out=CIN, in_=a_hcm[:])], dma=1, chan="c_bnd")
    coll = p.s(G, [cin_d, wload], lambda q: q.collective_compute(
        "AllReduce", ALU.add, replica_groups=[[0, 1], [2, 3], [4, 5], [6, 7]],
        ins=[CIN.opt()], outs=[COUT.opt()]))
    hback = p.s(Sy, [coll], lambda q: [q.dma_start(out=g_H[:], in_=COUT)], dma=1, chan="c_bnd")
    h_v = p.s(V, [hback], lambda q: q.tensor_scalar(
        out=g_H[:], in0=g_H[:], scalar1=g_mo[:], scalar2=None, op0=ALU.mult))

    corr_prev_ev = None
    corr_sp_prev = None
    corr_ldguard = [None, None]
    cb_bank_free = [None, None]
    for i in range(NTA):
        t0 = i * WA
        par = i % 2
        ldc = p.s(Sy, [h_v, corr_ldguard[par], spill_prev],
                  lambda q, t0=t0, par=par: [
            q.dma_start(out=a_acbf[par][:].rearrange("p (k t) -> p k t", k=DK),
                        in_=AC3[:, :, t0:t0 + WA])], dma=1, chan=f"c_cl{par}")
        cbs = []
        for j in range(DK):
            cbj = p.s(V, [ldc, h_v, corr_sp_prev], lambda q, j=j, par=par:
                      q.tensor_scalar(
                out=a_corrbf[cW(j)], in0=a_acbf[par][cW(j)],
                scalar1=g_H[:, j:j + 1], scalar2=None, op0=ALU.mult))
            cbs.append(cbj)
        corr_ldguard[par] = cbs[-1]
        evs = []
        for j in range(DK):
            bk = j % 2
            mm = p.s(T, [cbs[-1], cb_bank_free[bk]], lambda q, j=j, bk=bk: [
                q.matmul(pDW[bk][:, 0:WA],
                         a_wo[:, k * D + j * 128:k * D + j * 128 + 128],
                         a_corrbf[cW(k)], start=(k == 0), stop=(k == DK - 1))
                for k in range(DK)][-1])
            ev = p.s(V, [mm], lambda q, j=j, bk=bk: q.tensor_scalar(
                out=a_corrf[cW(j)], in0=pDW[bk][:, 0:WA], scalar1=0.5,
                scalar2=None, op0=ALU.mult))
            cb_bank_free[bk] = ev
            evs.append(ev)
        spc = p.s(Sy, [evs[-1]], lambda q, t0=t0: [
            q.dma_start(out=CR3[:, :, t0:t0 + WA],
                        in_=a_corrf[:].rearrange("p (k t) -> p k t", k=DK))], dma=1, chan="c_cs")
        corr_prev_ev = evs[-1]
        corr_sp_prev = spc

    if "CRSP" in tap_outs:
        corr_sp_prev = p.s(Sy, [corr_sp_prev], lambda q: [
            q.dma_start(out=tap_outs["CRSP"], in_=CRSP)], dma=1, chan="c_tap")
    if "APSP" in tap_outs:
        corr_sp_prev = p.s(Sy, [corr_sp_prev, spill_prev], lambda q: [
            q.dma_start(out=tap_outs["APSP"], in_=APSP)], dma=1, chan="c_tap")

    snapA = p.snap()
    barrierA = p.all_deps(snapA)
    ctxA.close()

    ctxB = ExitStack()
    sbB = lambda name, shape, dtype=F32: ctxB.enter_context(
        nc.sbuf_tensor(name, list(shape), dtype))
    g_combine = sbB("g_combine", (128, (TOK // 128) * E))
    n2bf = sbB("n2bf", (128, DK * TOK), BF16)
    out_acc = sbB("out_acc", (128, (TOK // 128) * D))
    ctxB1 = ExitStack()
    sbB1 = lambda name, shape, dtype=F32: ctxB1.enter_context(
        nc.sbuf_tensor(name, list(shape), dtype))
    b_mix = sbB1("b_mix", (128, DK * D), BF16)
    b_ap = sbB1("b_ap", (128, DK * WA))
    b_crr = sbB1("b_crr", (128, DK * WA))
    b_stats = sbB1("b_stats", (128, 2 * DK * WA), BF16)
    b_x1 = sbB1("b_x1", (128, DK * WA))
    b_n1 = sbB1("b_n1", (128, DK * WA))
    b_n1bf = sbB1("b_n1bf", (128, DK * WA), BF16)
    b_x2 = sbB1("b_x2", (128, DK * WA))
    b_n2f = sbB1("b_n2f", (128, DK * WA))
    b_rows = sbB1("b_rows", (1, 8 * WA))
    b_zr = sbB1("b_zr", (E, WA))
    b_ztm = sbB1("b_ztm", (128, 2 * E))
    b_srt = sbB1("b_srt", (128, 2 * E))
    b_sgn = sbB1("b_sgn", (128, E))
    b_ez = sbB1("b_ez", (128, 2 * E))
    b_nm = sbB1("b_nm", (128, 4))
    b_cebf = sbB1("b_cebf", (E, 2 * 128), BF16)

    ctxBp = ExitStack()
    psB = lambda name, shape: ctxBp.enter_context(
        nc.psum_tensor(name, list(shape), F32))
    pA_ = psB("pA_", [128, 512])
    pB_ = psB("pB_", [128, 512])
    pRow = psB("pRow", [128, 512])
    pBC = psB("pBC", [128, 512])
    pBC2 = psB("pBC2", [128, 512])
    pT_ = psB("pT_", [128, 512])
    pB2 = psB("pB2", [128, 1024])

    def finish_dummy(dep, pre_close=()):
        for c in pre_close:
            c.close()
        mz = p.s(V, [dep] + p.all_deps(), lambda q: q.memset(out_acc[:], 0.0))
        ods = []
        prev = mz
        for blk in range(TOK // 128):
            od = p.s(Sy, [prev], lambda q, blk=blk: [
                q.dma_start(out=OUT[blk * 128:(blk + 1) * 128, :],
                            in_=out_acc[:, blk * D:(blk + 1) * D])], dma=1,
                chan="c_out")
            prev = od
            ods.append(od)
        p.s(Sy, [ods[-1]], lambda q: q.nop())
        block = ctx.enter_context(nc.Block())
        p.emit(block)
        ctxB.close()
        ctx.close()
        return nc

    if upto in ("ld", "ldA", "ldB"):
        return finish_dummy(None, (ctxBp, ctxB1))
    if upto == "A":
        return finish_dummy(None, (ctxBp, ctxB1))

    mix_ld = p.s(G, barrierA, lambda q: [
        q.dma_start(out=b_mix[:].rearrange("p (k n) -> p k n", k=DK), in_=mixw.rearrange("(k p) n -> p k n", p=128))],
        dma=1, chan="c_mx")

    ap_prev = None
    n2_last = None
    comb_last = None
    oacc_last = None
    pT_free = None
    pB2_free = None
    pBC_free = None
    pBC2_free = None
    pRow_free = None
    mixbank_free = [None, None]
    for i in range(NTA):
        t0 = i * WA
        ldap = p.s(Sy, ([ap_prev] if ap_prev else []) + (barrierA if i == 0 else []),
                   lambda q, t0=t0: [
            q.dma_start(out=b_ap[:].rearrange("p (k t) -> p k t", k=DK),
                        in_=AP3[:, :, t0:t0 + WA]),
            q.dma_start(out=b_crr[:].rearrange("p (k t) -> p k t", k=DK),
                        in_=CR3[:, :, t0:t0 + WA])], dma=2, chan="c_b1l")
        apc = p.s(V, [ldap], lambda q: q.tensor_tensor(
            out=b_ap[:], in0=b_ap[:], in1=b_crr[:], op=ALU.add))
        apbf = p.s(Sc, [apc], lambda q: q.copy(
            out=b_stats[:, 0:DK * WA], in_=b_ap[:]))
        apsq = p.s(Sc, [apc], lambda q: q.activation(
            out=b_stats[:, DK * WA:2 * DK * WA], in_=b_ap[:], func=AF.Square))

        def ln_stats(dep_sq, dep_bf):
            st3 = b_stats[:].rearrange("p (h k t) -> p k h t", h=2, k=DK)
            pview = pRow[0:1, 0:2 * WA].rearrange("p (h t) -> p h t", h=2)
            return p.s(T, [dep_bf, dep_sq, pRow_free], lambda q: [
                q.matmul(pview, g_onescol[:], st3[:, k],
                         start=(k == 0), stop=(k == DK - 1)) for k in range(DK)][-1])

        st0 = ln_stats(apsq, apbf)
        r1a = p.s(V, [st0], lambda q: q.tensor_scalar(
            out=b_rows[:, 0:2 * WA], in0=pRow[0:1, 0:2 * WA],
            scalar1=1.0 / D, scalar2=None, op0=ALU.mult))
        r1b_ = p.s(V, [r1a], lambda q: q.tensor_tensor(
            out=b_rows[:, 6 * WA:7 * WA], in0=b_rows[:, 0:WA],
            in1=b_rows[:, 0:WA], op=ALU.mult))
        r1 = p.s(V, [r1b_], lambda q: q.tensor_tensor(
            out=b_rows[:, 2 * WA:3 * WA], in0=b_rows[:, WA:2 * WA],
            in1=b_rows[:, 6 * WA:7 * WA], op=ALU.subtract))
        pRow_free = r1a
        r2a = p.s(Sc, [r1, pBC_free], lambda q: q.activation(
            out=b_rows[:, 3 * WA:4 * WA], in_=b_rows[:, 2 * WA:3 * WA],
            func=AF.Ln, bias=g_cp[0:1, 1:2], scale=1.0))
        r2 = p.s(Sc, [r2a], lambda q: q.activation(
            out=b_rows[:, 3 * WA:4 * WA], in_=b_rows[:, 3 * WA:4 * WA],
            func=AF.Exp, scale=-0.5))
        r3 = p.s(V, [r2, pBC2_free], lambda q: q.tensor_tensor(
            out=b_rows[:, 4 * WA:5 * WA], in0=b_rows[:, 0:WA],
            in1=b_rows[:, 3 * WA:4 * WA], op=ALU.mult))
        r3x = p.s(V, [r3], lambda q: q.tensor_tensor(
            out=b_rows[:, 6 * WA:7 * WA], in0=b_rows[:, 3 * WA:4 * WA],
            in1=b_rows[:, 3 * WA:4 * WA], op=ALU.mult))
        r3y = p.s(V, [r3x], lambda q: q.tensor_tensor(
            out=b_rows[:, 6 * WA:7 * WA], in0=b_rows[:, 6 * WA:7 * WA],
            in1=b_rows[:, 2 * WA:3 * WA], op=ALU.mult))
        r4a = p.s(Sc, [r3y], lambda q: q.activation(
            out=b_rows[:, 5 * WA:6 * WA], in_=b_rows[:, 6 * WA:7 * WA],
            func=AF.Ln, bias=g_cp[0:1, 1:2], scale=1.0))
        r4 = p.s(Sc, [r4a], lambda q: q.activation(
            out=b_rows[:, 5 * WA:6 * WA], in_=b_rows[:, 5 * WA:6 * WA],
            func=AF.Exp, scale=-0.5))
        bc1 = p.s(T, [r3, pBC_free], lambda q: q.matmul(
            pBC[:, 0:2 * WA], g_ones1[:], b_rows[:, 3 * WA:5 * WA],
            start=True, stop=True))
        bc2 = p.s(T, [r4, pBC2_free], lambda q: q.matmul(
            pBC2[:, 0:WA], g_ones1[:], b_rows[:, 5 * WA:6 * WA],
            start=True, stop=True))
        x1s, n1s = [], []
        for j in range(DK):
            x1a = p.s(V, [bc1, apc], lambda q, j=j: q.tensor_tensor(
                out=b_x1[cW(j)], in0=b_ap[cW(j)], in1=pBC[:, 0:WA], op=ALU.mult))
            x1j = p.s(V, [x1a], lambda q, j=j: q.tensor_tensor(
                out=b_x1[cW(j)], in0=b_x1[cW(j)], in1=pBC[:, WA:2 * WA],
                op=ALU.subtract))
            n1a = p.s(V, [x1j, bc2], lambda q, j=j: q.tensor_tensor(
                out=b_n1[cW(j)], in0=b_x1[cW(j)], in1=pBC2[:, 0:WA], op=ALU.mult))
            n1j = p.s(V, [n1a], lambda q, j=j: q.tensor_scalar(
                out=b_n1[cW(j)], in0=b_n1[cW(j)], scalar1=g_g1[:, j:j + 1],
                scalar2=g_be1[:, j:j + 1], op0=ALU.mult, op1=ALU.add))
            x1s.append(x1j)
            n1s.append(n1j)
        pBC_free = n1s[-1]
        pBC2_free = n1s[-1]
        n1bf = p.s(Sc, [n1s[-1]], lambda q: q.copy(out=b_n1bf[:], in_=b_n1[:]))
        x2s = []
        for j in range(DK):
            bk = j % 2
            pm = [pA_, pB_][bk]
            mm = p.s(T, [n1bf, mixbank_free[bk], mix_ld], lambda q, j=j, pm=pm: [
                q.matmul(pm[:, 0:WA], b_mix[:, k * D + j * 128:k * D + j * 128 + 128],
                         b_n1bf[cW(k)], start=(k == 0), stop=(k == DK - 1))
                for k in range(DK)][-1])
            x2a = p.s(V, [mm], lambda q, j=j, pm=pm: q.scalar_tensor_tensor(
                out=b_x2[cW(j)], in0=pm[:, 0:WA], scalar=g_mixb[:, j:j + 1],
                in1=b_n1[cW(j)], op0=ALU.add, op1=ALU.add))
            x2j = p.s(V, [x2a], lambda q, j=j: q.tensor_tensor(
                out=b_x2[cW(j)], in0=b_x2[cW(j)], in1=b_x1[cW(j)], op=ALU.add))
            mixbank_free[bk] = x2j
            x2s.append(x2j)
        x2bf = p.s(Sc, [x2s[-1], st0], lambda q: q.copy(
            out=b_stats[:, 0:DK * WA], in_=b_x2[:]))
        x2sq = p.s(Sc, [x2s[-1], st0], lambda q: q.activation(
            out=b_stats[:, DK * WA:2 * DK * WA], in_=b_x2[:], func=AF.Square))
        st0b = ln_stats(x2sq, x2bf)
        r1b0 = p.s(V, [st0b], lambda q: q.tensor_scalar(
            out=b_rows[:, 0:2 * WA], in0=pRow[0:1, 0:2 * WA],
            scalar1=1.0 / D, scalar2=None, op0=ALU.mult))
        r1b1 = p.s(V, [r1b0], lambda q: q.tensor_tensor(
            out=b_rows[:, 6 * WA:7 * WA], in0=b_rows[:, 0:WA],
            in1=b_rows[:, 0:WA], op=ALU.mult))
        r1b = p.s(V, [r1b1], lambda q: q.tensor_tensor(
            out=b_rows[:, 2 * WA:3 * WA], in0=b_rows[:, WA:2 * WA],
            in1=b_rows[:, 6 * WA:7 * WA], op=ALU.subtract))
        pRow_free = r1b0
        r2b0 = p.s(Sc, [r1b, bc1, pBC_free], lambda q: q.activation(
            out=b_rows[:, 3 * WA:4 * WA], in_=b_rows[:, 2 * WA:3 * WA],
            func=AF.Ln, bias=g_cp[0:1, 1:2], scale=1.0))
        r2b = p.s(Sc, [r2b0], lambda q: q.activation(
            out=b_rows[:, 3 * WA:4 * WA], in_=b_rows[:, 3 * WA:4 * WA],
            func=AF.Exp, scale=-0.5))
        r3b = p.s(V, [r2b, bc1], lambda q: q.tensor_tensor(
            out=b_rows[:, 4 * WA:5 * WA], in0=b_rows[:, 0:WA],
            in1=b_rows[:, 3 * WA:4 * WA], op=ALU.mult))
        bc1b = p.s(T, [r3b, pBC_free], lambda q: q.matmul(
            pBC[:, 0:2 * WA], g_ones1[:], b_rows[:, 3 * WA:5 * WA],
            start=True, stop=True))
        n2s = []
        for j in range(DK):
            n2a = p.s(V, [bc1b, x2s[-1]], lambda q, j=j: q.tensor_tensor(
                out=b_n2f[cW(j)], in0=b_x2[cW(j)], in1=pBC[:, 0:WA], op=ALU.mult))
            n2b = p.s(V, [n2a], lambda q, j=j: q.tensor_tensor(
                out=b_n2f[cW(j)], in0=b_n2f[cW(j)], in1=pBC[:, WA:2 * WA],
                op=ALU.subtract))
            n2c = p.s(V, [n2b], lambda q, j=j: q.tensor_scalar(
                out=b_n2f[cW(j)], in0=b_n2f[cW(j)], scalar1=g_g2[:, j:j + 1],
                scalar2=g_be2[:, j:j + 1], op0=ALU.mult, op1=ALU.add))
            n2j = p.s(Sc, [n2c], lambda q, j=j, t0=t0: q.copy(
                out=n2bf[:, j * TOK + t0:j * TOK + t0 + WA], in_=b_n2f[cW(j)]))
            n2s.append(n2j)
        pBC_free = n2s[-1]
        n2_last = n2s[-1]
        rz = p.s(T, [n2s[-1], pT_free], lambda q: [
            q.matmul(pT_[0:E, 0:WA], g_rw[:, k * E:(k + 1) * E], b_n2f[cW(k)],
                     start=(k == 0), stop=(k == DK - 1)) for k in range(DK)][-1])
        rze = p.s(V, [rz], lambda q: q.tensor_copy(out=b_zr[:], in_=pT_[0:E, 0:WA]))
        ztr = p.s(T, [rze], lambda q: [
            q.transpose(pT_[:, 256 + s * E:256 + (s + 1) * E],
                        b_zr[:, s * 128:(s + 1) * 128], g_ident[0:E, 0:E])
            for s in range(2)][-1])
        zev = p.s(V, [ztr], lambda q: q.tensor_copy(
            out=b_ztm[:], in_=pT_[:, 256:256 + 2 * E]))
        pT_free = zev
        prev_cmb = None
        x12s = [None] * DK
        for sblk in range(2):
            blk = 2 * i + sblk
            zs = (slice(None), slice(sblk * E, (sblk + 1) * E))
            mx = p.s(V, [zev], lambda q, zs=zs: q.max(out=b_srt[zs], in_=b_ztm[zs]))
            nm = p.s(V, [mx], lambda q, sblk=sblk: q.tensor_scalar(
                out=b_nm[:, 2 * sblk:2 * sblk + 2],
                in0=b_srt[:, sblk * E:sblk * E + 2], scalar1=-1.0, scalar2=None,
                op0=ALU.mult))
            ex = p.s(Sc, [nm], lambda q, zs=zs, sblk=sblk: q.activation(
                out=b_ez[zs], in_=b_ztm[zs], func=AF.Exp,
                bias=b_nm[:, 2 * sblk:2 * sblk + 1], scale=1.0))
            sg = p.s(Sc, [ex, prev_cmb], lambda q, zs=zs, sblk=sblk: q.sign(
                out=b_sgn[:], in_=b_ztm[zs],
                bias=b_nm[:, 2 * sblk + 1:2 * sblk + 2]))
            c0 = p.s(V, [sg], lambda q, zs=zs, sblk=sblk: q.reduce_sum(
                b_nm[:, 2 * sblk:2 * sblk + 1], b_ez[zs], axis=AXX))
            c1 = p.s(V, [c0], lambda q: q.tensor_scalar(
                out=b_sgn[:], in0=b_sgn[:], scalar1=1.0, scalar2=1.0,
                op0=ALU.add, op1=ALU.min))
            c2 = p.s(V, [c1], lambda q, zs=zs: q.tensor_tensor(
                out=b_ez[zs], in0=b_ez[zs], in1=b_sgn[:], op=ALU.mult))
            c3 = p.s(V, [c2], lambda q, sblk=sblk: q.reciprocal(
                out=b_nm[:, 2 * sblk:2 * sblk + 1],
                in_=b_nm[:, 2 * sblk:2 * sblk + 1]))
            cmb = p.s(V, [c3], lambda q, zs=zs, blk=blk, sblk=sblk: q.tensor_scalar(
                out=g_combine[:, blk * E:(blk + 1) * E], in0=b_ez[zs],
                scalar1=b_nm[:, 2 * sblk:2 * sblk + 1], scalar2=None,
                op0=ALU.mult))
            prev_cmb = cmb
            comb_last = cmb
            ct = p.s(T, [cmb, pT_free], lambda q, blk=blk: q.transpose(
                pT_[0:E, 256:384], g_combine[:, blk * E:(blk + 1) * E],
                g_ident[:]))
            ce = p.s(V, [ct], lambda q, sblk=sblk: q.tensor_copy(
                out=b_cebf[:, sblk * 128:(sblk + 1) * 128], in_=pT_[0:E, 256:384]))
            pT_free = ce
            xts = []
            xt_prev = None
            for j in range(DK):
                if sblk == 0:
                    x12s[j] = p.s(V, [x2s[j]], lambda q, j=j: q.tensor_tensor(
                        out=b_n1[cW(j)], in0=b_x1[cW(j)], in1=b_x2[cW(j)],
                        op=ALU.add))
                xt = p.s(T, [x12s[j], pT_free, xt_prev],
                         lambda q, j=j, sblk=sblk: q.transpose(
                             pT_[:, 128:256],
                             b_n1[:, j * WA + sblk * 128:j * WA + (sblk + 1) * 128],
                             g_ident[:]))
                xe = p.s(V, [xt], lambda q, j=j, blk=blk: q.tensor_copy(
                    out=out_acc[:, blk * D + j * 128:blk * D + (j + 1) * 128],
                    in_=pT_[:, 128:256]))
                xt_prev = xe
                xts.append(xe)
            pT_free = xts[-1]
            b2m = p.s(T, [ce, ldb2, pB2_free], lambda q, sblk=sblk: [
                q.matmul(pB2[:, hh * 512:(hh + 1) * 512],
                         b_cebf[:, sblk * 128:(sblk + 1) * 128],
                         g_b2bf[:, hh * 512:(hh + 1) * 512],
                         start=True, stop=True) for hh in range(2)][-1])
            b2e = p.s(V, [b2m, xts[-1]], lambda q, blk=blk: q.tensor_tensor(
                out=out_acc[:, blk * D:(blk + 1) * D],
                in0=out_acc[:, blk * D:(blk + 1) * D], in1=pB2[:, 0:1024],
                op=ALU.add))
            pB2_free = b2e
            oacc_last = b2e
        ap_prev = [x1s[-1], apbf, apsq]

    if "N2" in tap_outs:
        p.s(Sy, [n2_last], lambda q: [
            q.dma_start(out=tap_outs["N2"], in_=n2bf[:])], dma=1, chan="c_tap")
    if "CMB" in tap_outs:
        p.s(Sy, [comb_last], lambda q: [
            q.dma_start(out=tap_outs["CMB"], in_=g_combine[:])], dma=1, chan="c_tap")

    snapB1 = p.snap()
    barrierB1 = p.all_deps(snapB1)
    ctxBp.close()
    ctxB1.close()
    if upto == "B1":
        return finish_dummy(None)

    hbuf = sbB("hbuf", (128, 16 * TOK), BF16)
    w2fb = sbB("w2fb", (128, 16 * D), BF16)
    ctxM1 = ExitStack()
    w1t = [ctxM1.enter_context(nc.sbuf_tensor(f"w1t{i}", [128, DK * 2 * 128],
                                              BF16)) for i in range(2)]
    ctxMp = ExitStack()
    p2 = [ctxMp.enter_context(nc.psum_tensor(f"p2_{i}", [128, 1024], F32))
          for i in range(4)]

    E_range = [0] if upto in ("moe0", "p1", "p2nm") else range(E)
    FB_range = [0] if upto in ("moe0", "p1", "p2nm") else range(2)
    w1_free = [barrierB1, barrierB1]
    w2_prev = barrierB1
    pass1_bank_free = [barrierB1, barrierB1]
    pass2_last_evac = barrierB1
    hbuf_free = barrierB1
    for e in E_range:
        for fb in FB_range:
            w2l = None
            silu_last = None
            for fcg in range(8):
                wsel = fcg % 2
                w1l = p.s(G, [w1_free[wsel]], lambda q, e=e, fb=fb, fcg=fcg,
                          wsel=wsel: [
                    q.dma_start(
                        out=w1t[wsel][:].rearrange("p (k l f) -> p k l f",
                                                   k=DK, l=2),
                        in_=w1h[e, fb * 8 + fcg])], dma=1, chan=f"c_w1{wsel}")
                if fcg == 2:
                    w2l = p.s(G, [w2_prev, pass2_last_evac],
                              lambda q, e=e, fb=fb: [
                        q.dma_start(out=w2fb[:].rearrange("p (fc d) -> p fc d",
                                                          fc=16),
                                    in_=w2h[e, fb])], dma=1, chan="c_w2")
                    w2_prev = w2l
                for fcl in range(2):
                    fcg_l = 2 * fcg + fcl
                    fc_g = fb * 16 + fcg_l
                    for ts in range(4):
                        bk = fcg_l % 2
                        mm = p.s(T, [w1l, pass1_bank_free[bk], n2_last],
                                 lambda q, wsel=wsel, fcl=fcl, ts=ts, bk=bk: [
                            q.matmul(p2[0][:, bk * 512:bk * 512 + 512],
                                     w1t[wsel][:, k * 256 + fcl * 128:
                                               k * 256 + fcl * 128 + 128],
                                     n2bf[:, k * TOK + ts * 512:
                                          k * TOK + (ts + 1) * 512],
                                     start=(k == 0), stop=(k == DK - 1))
                            for k in range(DK)][-1])
                        sl = p.s(Sc, [mm, hbuf_free],
                                 lambda q, e=e, fc_g=fc_g, fcg_l=fcg_l, ts=ts,
                                 bk=bk: q.activation(
                                     out=hbuf[:, fcg_l * TOK + ts * 512:
                                              fcg_l * TOK + (ts + 1) * 512],
                                     in_=p2[0][:, bk * 512:bk * 512 + 512],
                                     func=AF.Silu,
                                     bias=g_b1[:, e * FC + fc_g:e * FC + fc_g + 1],
                                     scale=1.0))
                        pass1_bank_free[bk] = sl
                        silu_last = sl
                w1_free[wsel] = silu_last
            if upto == "p1":
                pass2_last_evac = silu_last
                continue
            new_evac = []
            prev_ev = [pass2_last_evac] * 4
            for rnd in range(4):
                for tb in range(4):
                    tbg = rnd * 4 + tb
                    mmtb = p.s(T, [silu_last, w2l, prev_ev[tb]],
                               lambda q, rnd=rnd, tb=tb: [
                        q.matmul(p2[tb][:, dc * 512:(dc + 1) * 512],
                                 hbuf[:, fcl * TOK + rnd * 512 + tb * 128:
                                      fcl * TOK + rnd * 512 + (tb + 1) * 128],
                                 w2fb[:, fcl * D + dc * 512:
                                      fcl * D + (dc + 1) * 512],
                                 start=(fcl == 0), stop=(fcl == 15))
                        for fcl in range(16) for dc in range(2)][-1])
                    ev = p.s(V, [mmtb, comb_last, oacc_last],
                             lambda q, tb=tb, tbg=tbg, e=e: q.scalar_tensor_tensor(
                                 out=out_acc[:, tbg * D:(tbg + 1) * D],
                                 in0=p2[tb][:, 0:1024],
                                 scalar=g_combine[:, tbg * E + e:tbg * E + e + 1],
                                 in1=out_acc[:, tbg * D:(tbg + 1) * D],
                                 op0=ALU.mult, op1=ALU.add))
                    prev_ev[tb] = ev
                    new_evac.append(ev)
            pass2_last_evac = new_evac[-1]
            pass1_bank_free = [new_evac[-1], new_evac[-1]]
            hbuf_free = new_evac[-1]
            if e == E - 1 and fb == 1:
                final_evacs = new_evac
    moe_done = pass2_last_evac
    if upto in ("moe0", "moe", "p1", "p2nm"):
        ctxM1.close()
        return finish_dummy(moe_done, (ctxMp,))

    if "OACC" in tap_outs:
        p.s(Sy, [moe_done], lambda q: [
            q.dma_start(out=tap_outs["OACC"], in_=out_acc[:])], dma=1, chan="c_tap")

    ctxM1.close()
    f_sq = sbB("f_sq", (128, D))
    f_row = sbB("f_row", (128, 8))
    f_out = sbB("f_out", (128, D))
    out_deps = []
    prev_rd = None
    prev_od = None
    for blk in range(TOK // 128):
        bs = (slice(None), slice(blk * D, (blk + 1) * D))
        sq = p.s(Sc, [final_evacs[blk], prev_rd], lambda q, bs=bs: q.activation(
            out=f_sq[:], in_=out_acc[bs], func=AF.Square))
        rd0 = p.s(V, [sq, prev_od], lambda q, bs=bs: (
            q.reduce_sum(f_row[:, 0:1], out_acc[bs], axis=AXX),
            q.reduce_sum(f_row[:, 1:2], f_sq[:], axis=AXX))[-1])
        rd1 = p.s(V, [rd0], lambda q: q.tensor_scalar(
            out=f_row[:, 0:2], in0=f_row[:, 0:2], scalar1=1.0 / D,
            scalar2=None, op0=ALU.mult))
        rd2 = p.s(V, [rd1], lambda q: q.tensor_tensor(
            out=f_row[:, 2:3], in0=f_row[:, 0:1], in1=f_row[:, 0:1],
            op=ALU.mult))
        rd = p.s(V, [rd2], lambda q: q.tensor_tensor(
            out=f_row[:, 2:3], in0=f_row[:, 1:2], in1=f_row[:, 2:3],
            op=ALU.subtract))
        rs0 = p.s(Sc, [rd], lambda q: q.activation(
            out=f_row[:, 3:4], in_=f_row[:, 2:3], func=AF.Ln,
            bias=g_cp[:, 1:2], scale=1.0))
        rs = p.s(Sc, [rs0], lambda q: q.activation(
            out=f_row[:, 3:4], in_=f_row[:, 3:4], func=AF.Exp, scale=-0.5))
        fo = p.s(V, [rs, prev_od], lambda q, bs=bs: q.tensor_scalar(
            out=f_out[:], in0=out_acc[bs], scalar1=f_row[:, 0:1],
            scalar2=f_row[:, 3:4], op0=ALU.subtract, op1=ALU.mult))
        od = p.s(Sy, [fo], lambda q, blk=blk: [
            q.dma_start(out=OUT[blk * 128:(blk + 1) * 128, :], in_=f_out[:])],
            dma=1, chan="c_out")
        prev_rd = rd
        prev_od = od
        out_deps.append(od)

    tapdep = [("c_tap", p.counts["c_tap"])] if "c_tap" in p.counts else []
    p.s(Sy, [out_deps[-1]] + tapdep, lambda q: q.nop())

    block = ctx.enter_context(nc.Block())
    p.emit(block)
    ctxMp.close()
    ctxB.close()
    ctx.close()
    return nc


def _host_inputs(inputs):
    x = np.asarray(inputs["L0_attn_x"], np.float32)
    pos = np.arange(S, dtype=np.float64)[:, None]
    div = np.exp(np.arange(0, PE_DIM, 2, dtype=np.float64) *
                 (-np.log(10000.0) / PE_DIM))
    pe = np.zeros((S, PE_DIM), np.float64)
    pe[:, 0::2] = np.sin(pos * div)
    pe[:, 1::2] = np.cos(pos * div)
    pe = (pe / 100.0).astype(np.float32)

    v8 = lambda v: np.ascontiguousarray(
        np.asarray(v, np.float32).reshape(DK, 128).T)
    Wdx = np.asarray(inputs["L0_attn_W_dx"], np.float32)
    Wx = np.asarray(inputs["L0_attn_W_x"], np.float32)
    w1 = np.asarray(inputs["L0_ffn_w1"], np.float32)
    b1 = np.asarray(inputs["L0_ffn_b1"], np.float32)
    w1hh = np.ascontiguousarray(
        w1.reshape(E, DK, 128, 16, 2, 128).transpose(0, 3, 2, 1, 4, 5))
    b1t = np.ascontiguousarray(b1.reshape(E, FC, 128).transpose(2, 0, 1)
                               .reshape(128, E * FC))

    shared = dict(
        Wdx_m=np.ascontiguousarray(Wdx[:D]),
        Wdx_p=np.ascontiguousarray(Wdx[D:]),
        Wx_m=np.ascontiguousarray(Wx[:D]),
        Wx_p=np.ascontiguousarray(Wx[D:]),
        Wgx=np.asarray(inputs["L0_attn_W_gx"], np.float32),
        Wga=np.asarray(inputs["L0_attn_W_ga"], np.float32),
        Wo=np.asarray(inputs["L0_attn_W_o"], np.float32),
        mixw=np.asarray(inputs["L0_ffn_mix_w"], np.float32),
        rw=np.asarray(inputs["L0_ffn_router_w"], np.float32),
        bdx=v8(inputs["L0_attn_b_dx"]), bx=v8(inputs["L0_attn_b_x"]),
        bgx=v8(inputs["L0_attn_b_gx"]), bga=v8(inputs["L0_attn_b_ga"]),
        bo=v8(inputs["L0_attn_b_o"]), mixb=v8(inputs["L0_ffn_mix_b"]),
        alog=v8(inputs["L0_attn_a_logit"]),
        g1=v8(inputs["L0_ffn_g1"]), be1=v8(inputs["L0_ffn_be1"]),
        g2=v8(inputs["L0_ffn_g2"]), be2=v8(inputs["L0_ffn_be2"]),
        b1h=b1t, b2h=np.asarray(inputs["L0_ffn_b2"], np.float32),
        w1h=w1hh,
        w2h=np.ascontiguousarray(
            np.asarray(inputs["L0_ffn_w2"], np.float32)
            .reshape(E, 2, 16, 128, D).transpose(0, 1, 3, 2, 4)),
        ident=np.eye(128, dtype=np.float32),
    )
    in_maps = []
    for c in range(8):
        b, h = c // 2, c % 2
        d = dict(shared)
        d["xT"] = np.ascontiguousarray(x[b, h * TOK:(h + 1) * TOK, :].T)
        d["peT"] = np.ascontiguousarray(pe[h * TOK:(h + 1) * TOK, :].T)
        d["me"] = np.full((128, 1), 1.0 if h == 0 else 0.0, np.float32)
        d["mo"] = np.full((128, 1), 1.0 if h == 1 else 0.0, np.float32)
        in_maps.append(d)
    return in_maps


def kernel(**inputs):
    in_maps = _host_inputs(inputs)
    nc = build_program()
    res = run_bass_kernel_spmd(nc, in_maps, core_ids=list(range(8)))
    out = np.empty((B, S, D), np.float32)
    for c in range(8):
        b, h = c // 2, c % 2
        out[b, h * TOK:(h + 1) * TOK, :] = res.results[c]["OUT"]
    return out
